# version 02
# baseline (speedup 1.0000x reference)
import math
from contextlib import ExitStack

import numpy as np
import concourse.bass as bass
import concourse.mybir as mybir
from concourse.bass_utils import run_bass_kernel_spmd

F32 = mybir.dt.float32
BF16 = mybir.dt.bfloat16
I32 = mybir.dt.int32
AF = mybir.ActivationFunctionType
ALU = mybir.AluOpType
AX = mybir.AxisListType

NCORES = 8
B_LOC = 2
S = 2048
D = 1024
T_LOC = B_LOC * S
NCH = D // 128


class Sem:
    def __init__(self, p, name):
        self.h = p.es.enter_context(p.nc.semaphore(name))
        self.n = 0

    def dma(self, ins):
        ins.then_inc(self.h, 16)
        self.n += 16
        return self.n

    def inc(self, ins, k=1):
        ins.then_inc(self.h, k)
        self.n += k
        return self.n


class Prog:
    def __init__(self, name):
        self.nc = bass.Bass("TRN2", target_bir_lowering=False)
        self.es = ExitStack()
        self.name = name

    def inp(self, name, shape, dtype=F32):
        return self.nc.dram_tensor(name, list(shape), dtype, kind="ExternalInput").ap()

    def out(self, name, shape, dtype=F32):
        return self.nc.dram_tensor(name, list(shape), dtype, kind="ExternalOutput").ap()

    def dram(self, name, shape, dtype):
        return self.nc.dram_tensor(name, list(shape), dtype, kind="Internal").ap()

    def sb(self, name, shape, dtype):
        return self.es.enter_context(self.nc.sbuf_tensor(name, list(shape), dtype))

    def ps(self, name, shape, dtype=F32):
        return self.es.enter_context(self.nc.psum_tensor(name, list(shape), dtype))

    def sem(self, name):
        return Sem(self, name)


def host_consts_common():
    c = {}
    c["C_ident"] = np.eye(128, dtype=np.float32)
    c["C_ones_col"] = np.ones((128, 1), dtype=np.float32)
    c["C_ones_row"] = np.ones((1, 128), dtype=np.float32)
    for k in range(4):
        jj = np.arange(128)[:, None] + 128 * k
        ii = np.arange(512)[None, :]
        c[f"C_dmask{k}"] = (jj <= ii).astype(np.float32)
    return c


def host_consts_l1():
    NB, MAXD = 32, 128
    d = np.arange(2048)
    max_exact = NB // 2
    n_safe = np.maximum(d, 1).astype(np.float64)
    val_large = max_exact + (np.log(n_safe / max_exact) / math.log(MAXD / max_exact)
                             * (NB - max_exact)).astype(np.int64)
    val_large = np.minimum(val_large, NB - 1)
    bucket = np.where(d < max_exact, d, val_large)
    oh = np.zeros((NB, 2048), np.float32)
    oh[bucket, d] = 1.0
    return {"C_oht": oh, "C_rev": np.fliplr(np.eye(128)).astype(np.float32).copy()}


def host_consts_l0():
    c = {}
    dh = 64
    inv = 1.0 / (10000.0 ** (np.arange(0, dh, 2, dtype=np.float64) / dh))
    t = np.arange(S, dtype=np.float64)
    fr = np.einsum("t,d->td", t, inv)
    fr = np.concatenate([fr, fr], axis=1)
    cos = np.cos(fr).astype(np.float32)
    sin = np.sin(fr).astype(np.float32)
    sins = sin.copy()
    sins[:, :32] *= -1.0
    c["C_cos"] = np.ascontiguousarray(np.tile(cos.T, (2, 1)))
    c["C_sins"] = np.ascontiguousarray(np.tile(sins.T, (2, 1)))
    P = np.zeros((128, 128), dtype=np.float32)
    for m in range(128):
        k = m + 32 if (m % 64) < 32 else m - 32
        P[k, m] = 1.0
    c["C_perm"] = P
    for k in range(4):
        jj = np.arange(128)[:, None] + 128 * k
        ii = np.arange(512)[None, :]
        c[f"C_dmask{k}"] = (jj <= ii).astype(np.float32)
    return c


def emit_wT_prep(p, blk, wbf_hbm, wT_sb_chunks, sem_w, o_lo, o_hi, wait=None):
    O, Din = wbf_hbm.shape

    @blk.sync
    def _(sync):
        if wait is not None:
            sync.wait_ge(wait[0].h, wait[1])
        for dch in range(Din // 128):
            sync.dma_start(out=wT_sb_chunks[dch][:, 0:o_hi - o_lo],
                           in_=wbf_hbm[o_lo:o_hi, dch * 128:(dch + 1) * 128],
                           transpose=True).then_inc(sem_w.h, 16)
            sem_w.n += 16


def emit_layernorm(p, blk, x_ch, lnx_ch, sems, cst, psb, tag, t_lo=0, t_hi=T_LOC):
    sem_mm, sem_v, sem_a = sems["mm"], sems["v"], sems["a"]
    ps_m, ps_v2, ps_A, ps_B = psb["row"], psb["row2"], psb["main"], psb["aux"]
    if not hasattr(p, "_ln_scratch"):
        p._ln_scratch = (p.sb("ln_x2", [128, 512], F32),
                         p.sb("ln_mrow", [1, 512], F32),
                         p.sb("ln_vrow", [1, 512], F32),
                         p.sb("ln_arow", [1, 512], F32),
                         p.sb("ln_brow", [1, 512], F32))
    x2, mrow, vrow, arow, brow = p._ln_scratch

    for col in range((t_hi - t_lo) // 512):
        c0 = t_lo + col * 512
        for ch in range(NCH):
            @blk.scalar
            def _(s, ch=ch, c0=c0):
                s.wait_ge(sem_mm.h, sem_mm.n)
                s.activation(out=x2[:], in_=x_ch[ch][:, c0:c0 + 512],
                             func=AF.Square).then_inc(sem_a.h, 1)
                sem_a.n += 1

            @blk.tensor
            def _(t, ch=ch, c0=c0):
                t.wait_ge(sem_a.h, sem_a.n)
                t.matmul(ps_m[:], cst["ones_col"][:], x_ch[ch][:, c0:c0 + 512],
                         start=(ch == 0), stop=(ch == NCH - 1))
                t.matmul(ps_v2[:], cst["ones_col"][:], x2[:],
                         start=(ch == 0), stop=(ch == NCH - 1)).then_inc(sem_mm.h, 1)
                sem_mm.n += 1

        @blk.vector
        def _(v, c0=c0):
            v.wait_ge(sem_mm.h, sem_mm.n)
            v.tensor_scalar(out=mrow[:], in0=ps_m[:], scalar1=1.0 / D, scalar2=None,
                            op0=ALU.mult)
            v.tensor_scalar(out=vrow[:], in0=ps_v2[:], scalar1=1.0 / D, scalar2=None,
                            op0=ALU.mult)
            v.tensor_mul(out=arow[:], in0=mrow[:], in1=mrow[:])
            v.tensor_sub(out=vrow[:], in0=vrow[:], in1=arow[:]).then_inc(sem_v.h, 1)
            sem_v.n += 1

        @blk.scalar
        def _(s, c0=c0):
            s.wait_ge(sem_v.h, sem_v.n)
            s.activation(out=vrow[:], in_=vrow[:], func=AF.Sqrt,
                         bias=cst["eps_col"][0:1, :], scale=1.0).then_inc(sem_a.h, 1)
            sem_a.n += 1

        @blk.vector
        def _(v, c0=c0):
            v.wait_ge(sem_a.h, sem_a.n)
            v.reciprocal(out=arow[:], in_=vrow[:])
            v.tensor_mul(out=brow[:], in0=mrow[:], in1=arow[:])
            v.tensor_scalar(out=brow[:], in0=brow[:], scalar1=-1.0, scalar2=None,
                            op0=ALU.mult).then_inc(sem_v.h, 1)
            sem_v.n += 1

        @blk.tensor
        def _(t, c0=c0):
            t.wait_ge(sem_v.h, sem_v.n)
            t.matmul(ps_A[:], cst["ones_row"][:], arow[:], start=True, stop=True)
            t.matmul(ps_B[:], cst["ones_row"][:], brow[:], start=True,
                     stop=True).then_inc(sem_mm.h, 1)
            sem_mm.n += 1

        for ch in range(NCH):
            @blk.vector
            def _(v, ch=ch, c0=c0):
                v.wait_ge(sem_mm.h, sem_mm.n)
                v.tensor_mul(out=x2[:], in0=x_ch[ch][:, c0:c0 + 512], in1=ps_A[:])
                ins = v.tensor_add(out=lnx_ch[ch][:, c0:c0 + 512], in0=x2[:],
                                   in1=ps_B[:])
                if ch == NCH - 1:
                    ins.then_inc(sem_v.h, 1)
                    sem_v.n += 1
        @blk.scalar
        def _(s):
            s.wait_ge(sem_v.h, sem_v.n)


DBG = None


def build_l0_attn():
    p = Prog("l0_attn")
    nc = p.nc
    x_hbm = p.inp("x_fm", [NCH, 128, T_LOC])
    qkv_w = p.inp("L0_attn_qkv_w", [3 * D, D])
    qkv_b = p.inp("L0_attn_qkv_b", [3 * D])
    proj_w = p.inp("L0_attn_proj_w", [D, D])
    proj_b = p.inp("L0_attn_proj_b", [D])
    c_cos = p.inp("C_cos", [128, S])
    c_sins = p.inp("C_sins", [128, S])
    c_perm = p.inp("C_perm", [128, 128])
    c_onescol = p.inp("C_ones_col", [128, 1])
    c_onesrow = p.inp("C_ones_row", [1, 128])
    c_dmask = [p.inp(f"C_dmask{k}", [128, 512]) for k in range(4)]
    c_ident = p.inp("C_ident", [128, 128])
    xout = p.out("xout_fm", [NCH, 128, T_LOC])

    wbf = p.dram("wbf_qkv", [3 * D, D], BF16)
    pbf = p.dram("wbf_proj", [D, D], BF16)
    vbuf = p.dram("vbuf", [16, 128, D], BF16)

    ps_main = p.ps("ps_main", [128, 512])
    ps_aux = p.ps("ps_aux", [128, 512])
    ps_sim = p.ps("ps_sim", [128, 512])
    ps_row = p.ps("ps_row", [1, 512])
    ps_row2 = p.ps("ps_row2", [1, 512])
    ps_col = p.ps("ps_col", [128, 1])
    ps_ctx = p.ps("ps_ctx", [128, 64])
    ps_tr = p.ps("ps_tr", [128, 128], BF16)

    arena = p.sb("arena", [128, 16 * 1024], F32)
    arena_b = arena[:].bitcast(BF16)
    x_ch = [arena[:, c * 2048:(c + 1) * 2048] for c in range(NCH)]
    qf = [arena_b[:, c * 2048:(c + 1) * 2048] for c in range(NCH)]
    kf = [arena_b[:, 16384 + c * 2048:16384 + (c + 1) * 2048] for c in range(NCH)]
    lnx = [p.sb(f"lnx{c}", [128, S], BF16) for c in range(NCH)]
    ctx_tm = p.sb("ctx_tm", [128, 16 * 1024], BF16)
    etile = p.sb("etile", [128, 16 * 512], BF16)
    cfm = [p.sb(f"cfm{c}", [128, 512], BF16) for c in range(NCH)]
    v_h = p.sb("v_h", [128, 16 * 64], BF16)
    wsl = [p.sb(f"wsl{c}", [128, 512], BF16) for c in range(NCH)]
    cosb = p.sb("cosb", [128, S], BF16)
    sinsb = p.sb("sinsb", [128, S], BF16)
    permb = p.sb("permb", [128, 128], BF16)
    dmaskb = [p.sb(f"dmask{k}", [128, 512], BF16) for k in range(4)]
    identb = p.sb("identb", [128, 128], BF16)
    ones_col = p.sb("ones_col", [128, 1], F32)
    ones_row = p.sb("ones_row", [1, 128], F32)
    ones_col_b = p.sb("ones_col_b", [128, 1], BF16)
    ones_row_b = p.sb("ones_row_b", [1, 128], BF16)
    qkb_sb = [p.sb(f"qkb{c}", [128, 1], F32) for c in range(16)]
    vb_row = p.sb("vb_row", [1, D], BF16)
    projb_sb = [p.sb(f"projb{c}", [128, 1], F32) for c in range(NCH)]
    qtmp = p.sb("qtmp", [128, 512], BF16)
    r1 = p.sb("r1", [128, 512], BF16)
    r2 = p.sb("r2", [128, 512], BF16)
    q2col = p.sb("q2col", [128, 512], BF16)
    vstage = p.sb("vstage", [128, 512], BF16)
    vbias_t = p.sb("vbias_t", [128, 512], BF16)
    nmx = p.sb("nmx", [1, 32], F32)
    negbq = p.sb("negbq", [1, 16], BF16)
    tmp11 = p.sb("tmp11", [1, 1], F32)
    den_sb = p.sb("den_sb", [1, 512], F32)
    negbq_col = p.sb("negbq_col", [128, 1], F32)
    recip_col = p.sb("recip_col", [128, 1], F32)
    dtile = p.sb("dtile", [128, 512], F32)

    sem_x = p.sem("sx")
    sem_g = p.sem("sg")
    sem_w = p.sem("sw")
    sem_mm = p.sem("smm")
    sem_v = p.sem("sv")
    sem_a = p.sem("sa")
    sem_o = p.sem("so")
    sems = {"mm": sem_mm, "v": sem_v, "a": sem_a}

    def wslice_load(blk, src_bf, o0, width, mm_guard):
        @blk.sync
        def _(sync):
            sync.wait_ge(sem_mm.h, mm_guard)
            sync.wait_ge(sem_g.h, wcast_done)
            for dch in range(NCH):
                sync.dma_start(out=wsl[dch][:, 0:width],
                               in_=src_bf[o0:o0 + width, dch * 128:(dch + 1) * 128],
                               transpose=True).then_inc(sem_w.h, 16)
                sem_w.n += 16

    with nc.Block() as blk:
        @blk.gpsimd
        def _(g):
            for dst, src in [(cosb, c_cos), (sinsb, c_sins), (permb, c_perm),
                             (ones_col_b, c_onescol), (ones_row_b, c_onesrow),
                             (identb, c_ident)]:
                g.dma_start(out=dst[:], in_=src).then_inc(sem_g.h, 16)
                sem_g.n += 16
            for k in range(4):
                g.dma_start(out=dmaskb[k][:], in_=c_dmask[k]).then_inc(sem_g.h, 16)
                sem_g.n += 16
            for c in range(16):
                g.dma_start(out=qkb_sb[c][:],
                            in_=qkv_b[c * 128:(c + 1) * 128].unsqueeze(1)).then_inc(sem_g.h, 16)
                sem_g.n += 16
            g.dma_start(out=vb_row[:], in_=qkv_b[2 * D:3 * D].unsqueeze(0)).then_inc(sem_g.h, 16)
            sem_g.n += 16
            for c in range(NCH):
                g.dma_start(out=projb_sb[c][:],
                            in_=proj_b[c * 128:(c + 1) * 128].unsqueeze(1)).then_inc(sem_g.h, 16)
                sem_g.n += 16
            g.dma_start(out=wbf, in_=qkv_w).then_inc(sem_g.h, 16)
            g.dma_start(out=pbf, in_=proj_w).then_inc(sem_g.h, 16)
            sem_g.n += 32

        @blk.sync
        def _(sync):
            sync.dma_start(out=ones_col[:], in_=c_onescol).then_inc(sem_x.h, 16)
            sync.dma_start(out=ones_row[:], in_=c_onesrow).then_inc(sem_x.h, 16)
            sem_x.n += 32
            for c in range(NCH):
                sync.dma_start(out=xout[c], in_=x_hbm[c]).then_inc(sem_o.h, 16)
                sem_o.n += 16

        eps_col = p.sb("eps_col", [128, 1], F32)

        @blk.vector
        def _(v):
            v.memset(eps_col[:], 1e-5).then_inc(sem_v.h, 1)
            sem_v.n += 1

        for eng in ("vector", "scalar", "tensor"):
            def mk(eng=eng):
                def f(e):
                    e.wait_ge(sem_x.h, sem_x.n)
                    e.wait_ge(sem_g.h, sem_g.n)
                    e.wait_ge(sem_v.h, sem_v.n)
                return f
            getattr(blk, eng)(mk())

        cst = {"ones_col": ones_col, "ones_row": ones_row, "eps_col": eps_col}
        psb = {"row": ps_row, "row2": ps_row2, "main": ps_main, "aux": ps_aux}
        wcast_done = sem_g.n

        def dump(tag, aps):
            if DBG != tag:
                return False
            outs = [p.out(f"dbg{i}", list(a.shape), a.dtype)
                    for i, a in enumerate(aps)]
            @blk.sync
            def _(sync):
                for sm in (sem_v, sem_a, sem_mm, sem_g, sem_o, sem_x, sem_w):
                    sync.wait_ge(sm.h, sm.n)
                for o, a in zip(outs, aps):
                    sync.dma_start(out=o, in_=a).then_inc(sem_x.h, 16)
                    sem_x.n += 16
                sync.wait_ge(sem_x.h, sem_x.n)
            return True

        accum_started = False
        for b in range(B_LOC):
            t0 = b * S

            @blk.sync
            def _(sync, t0=t0, guard=sem_mm.n):
                sync.wait_ge(sem_mm.h, guard)
                for c in range(NCH):
                    sync.dma_start(out=x_ch[c], in_=x_hbm[c][:, t0:t0 + S]
                                   ).then_inc(sem_x.h, 16)
                    sem_x.n += 16

            for eng in ("vector", "scalar", "tensor"):
                def mk(eng=eng):
                    def f(e):
                        e.wait_ge(sem_x.h, sem_x.n)
                    return f
                getattr(blk, eng)(mk())

            emit_layernorm(p, blk, x_ch, lnx, sems, cst, psb, f"l0a_b{b}",
                           t_lo=0, t_hi=S)
            if dump("ln", [lnx[c][:] for c in range(NCH)]):
                break

            n_og = 1 if DBG in ("qk1", "qk0") else 4
            n_ol = 1 if DBG == "qk0" else 4
            n_tc = 1 if DBG == "qk0" else 4
            for og in range(n_og):
                wslice_load(blk, wbf, og * 512, 512, sem_mm.n)
                for ol in range(n_ol):
                    och = og * 4 + ol
                    dst = qf[och] if och < 8 else kf[och - 8]
                    for tc in range(n_tc):
                        @blk.tensor
                        def _(t, ol=ol, tc=tc, ww=sem_w.n):
                            t.wait_ge(sem_w.h, ww)
                            t.wait_ge(sem_a.h, sem_a.n)
                            t.wait_ge(sem_v.h, sem_v.n)
                            for dch in range(NCH):
                                ins = t.matmul(ps_main[:], wsl[dch][:, ol * 128:(ol + 1) * 128],
                                               lnx[dch][:, tc * 512:(tc + 1) * 512],
                                               start=(dch == 0), stop=(dch == NCH - 1))
                            ins.then_inc(sem_mm.h, 1)
                            sem_mm.n += 1

                        @blk.scalar
                        def _(s, och=och):
                            s.wait_ge(sem_mm.h, sem_mm.n)
                            s.activation(out=qtmp[:], in_=ps_main[:], func=AF.Identity,
                                         bias=qkb_sb[och][:], scale=1.0).then_inc(sem_a.h, 1)
                            sem_a.n += 1

                        @blk.tensor
                        def _(t):
                            t.wait_ge(sem_a.h, sem_a.n)
                            t.matmul(ps_aux[:], permb[:], qtmp[:], start=True,
                                     stop=True).then_inc(sem_mm.h, 1)
                            sem_mm.n += 1

                        @blk.vector
                        def _(v, tc=tc, dst=dst):
                            v.wait_ge(sem_mm.h, sem_mm.n)
                            tc0 = tc * 512
                            v.tensor_mul(out=r1[:], in0=qtmp[:], in1=cosb[:, tc0:tc0 + 512])
                            v.tensor_mul(out=r2[:], in0=ps_aux[:], in1=sinsb[:, tc0:tc0 + 512])
                            v.tensor_add(out=dst[:, tc0:tc0 + 512], in0=r1[:],
                                         in1=r2[:]).then_inc(sem_v.h, 1)
                            sem_v.n += 1

                        @blk.scalar
                        def _(s):
                            s.wait_ge(sem_v.h, sem_v.n)

            if dump("qk", [qf[c] for c in range(NCH)] + [kf[c] for c in range(NCH)]):
                break
            if dump("qk1", [qf[c] for c in range(4)]):
                break
            if dump("qk0", [qf[0]]):
                break

            for och in range(2):
                wslice_load(blk, wbf, 2048 + och * 512, 512, sem_mm.n)

                @blk.tensor
                def _(t, och=och):
                    t.wait_ge(sem_a.h, sem_a.n)
                    t.matmul(ps_main[:], ones_row_b[:],
                             vb_row[:, och * 512:(och + 1) * 512],
                             start=True, stop=True).then_inc(sem_mm.h, 1)
                    sem_mm.n += 1

                @blk.scalar
                def _(s):
                    s.wait_ge(sem_mm.h, sem_mm.n)
                    s.activation(out=vbias_t[:], in_=ps_main[:],
                                 func=AF.Identity, scale=1.0).then_inc(sem_a.h, 1)
                    sem_a.n += 1

                for tt in range(16):
                    @blk.tensor
                    def _(t, tt=tt, ww=sem_w.n):
                        t.wait_ge(sem_w.h, ww)
                        t.wait_ge(sem_a.h, sem_a.n)
                        t.wait_ge(sem_v.h, sem_v.n)
                        for dch in range(NCH):
                            ins = t.matmul(ps_aux[:], lnx[dch][:, tt * 128:(tt + 1) * 128],
                                           wsl[dch][:, 0:512], start=(dch == 0),
                                           stop=(dch == NCH - 1))
                        ins.then_inc(sem_mm.h, 1)
                        sem_mm.n += 1

                    @blk.vector
                    def _(v, tt=tt):
                        v.wait_ge(sem_mm.h, sem_mm.n)
                        v.wait_ge(sem_g.h, sem_g.n)
                        v.tensor_add(out=vstage[:], in0=ps_aux[:],
                                     in1=vbias_t[:]).then_inc(sem_v.h, 1)
                        sem_v.n += 1

                    @blk.gpsimd
                    def _(g, tt=tt, och=och):
                        g.wait_ge(sem_v.h, sem_v.n)
                        g.dma_start(out=vbuf[tt][:, och * 512:(och + 1) * 512],
                                    in_=vstage[:]).then_inc(sem_g.h, 16)
                        sem_g.n += 16

            if dump("v", [vbuf]):
                break

            @blk.vector
            def _(v):
                v.wait_ge(sem_v.h, sem_v.n)
                v.memset(nmx[:], 0.0).then_inc(sem_v.h, 1)
                sem_v.n += 1

            for src_i, src in enumerate((qf, kf)):
                for ch in range(NCH):
                    for cc in range(4):
                        @blk.scalar
                        def _(s, src=src, ch=ch, cc=cc):
                            s.wait_ge(sem_mm.h, sem_mm.n)
                            s.activation(out=q2col[:],
                                         in_=src[ch][:, cc * 512:(cc + 1) * 512],
                                         func=AF.Square).then_inc(sem_a.h, 1)
                            sem_a.n += 1

                        for half in range(2):
                            @blk.tensor
                            def _(t, half=half):
                                t.wait_ge(sem_a.h, sem_a.n)
                                t.wait_ge(sem_v.h, sem_v.n)
                                t.matmul(ps_row[:],
                                         ones_col_b[half * 64:half * 64 + 64, :],
                                         q2col[half * 64:half * 64 + 64, :],
                                         start=True, stop=True).then_inc(sem_mm.h, 1)
                                sem_mm.n += 1

                            @blk.vector
                            def _(v, src_i=src_i, ch=ch, half=half):
                                v.wait_ge(sem_mm.h, sem_mm.n)
                                hh = src_i * 16 + ch * 2 + half
                                v.tensor_reduce(out=tmp11[:], in_=ps_row[:],
                                                axis=AX.X, op=ALU.max)
                                v.tensor_max(out=nmx[:, hh:hh + 1],
                                             in0=nmx[:, hh:hh + 1],
                                             in1=tmp11[:]).then_inc(sem_v.h, 1)
                                sem_v.n += 1

            @blk.vector
            def _(v):
                v.tensor_mul(out=nmx[:, 0:16], in0=nmx[:, 0:16],
                             in1=nmx[:, 16:32]).then_inc(sem_v.h, 1)
                sem_v.n += 1

            @blk.scalar
            def _(s):
                s.wait_ge(sem_v.h, sem_v.n)
                s.activation(out=nmx[:, 16:32], in_=nmx[:, 0:16],
                             func=AF.Sqrt).then_inc(sem_a.h, 1)
                sem_a.n += 1

            @blk.vector
            def _(v):
                v.wait_ge(sem_a.h, sem_a.n)
                v.tensor_scalar(out=negbq[:], in0=nmx[:, 16:32], scalar1=-0.125,
                                scalar2=None, op0=ALU.mult).then_inc(sem_v.h, 1)
                sem_v.n += 1

            if dump("norms", [nmx[:], negbq[:]]):
                break

            for h in range(16):
                ch, ro = h // 2, (h % 2) * 64

                @blk.sync
                def _(sync, h=h, guard=sem_mm.n):
                    sync.wait_ge(sem_mm.h, guard)
                    sync.wait_ge(sem_g.h, sem_g.n)
                    sync.dma_start(out=v_h[:].rearrange("p (a c) -> p a c", c=64),
                                   in_=vbuf[:, :, 64 * h:64 * h + 64].transpose(
                                       [1, 0, 2])).then_inc(sem_x.h, 16)
                    sem_x.n += 16

                @blk.tensor
                def _(t, h=h):
                    t.wait_ge(sem_v.h, sem_v.n)
                    t.matmul(ps_col[:], ones_row_b[:], negbq[:, h:h + 1],
                             start=True, stop=True).then_inc(sem_mm.h, 1)
                    sem_mm.n += 1

                @blk.vector
                def _(v):
                    v.wait_ge(sem_mm.h, sem_mm.n)
                    v.tensor_copy(out=negbq_col[:], in_=ps_col[:]).then_inc(sem_v.h, 1)
                    sem_v.n += 1

                for ich in range(4):
                    i0 = ich * 512
                    njt = 4 * ich + 4
                    for jt in range(njt):
                        @blk.tensor
                        def _(t, jt=jt, ch=ch, ro=ro, i0=i0):
                            t.wait_ge(sem_v.h, sem_v.n)
                            t.wait_ge(sem_a.h, sem_a.n)
                            t.matmul(ps_sim[:],
                                     kf[ch][ro:ro + 64, jt * 128:(jt + 1) * 128],
                                     qf[ch][ro:ro + 64, i0:i0 + 512],
                                     start=True, stop=True).then_inc(sem_mm.h, 1)
                            sem_mm.n += 1

                        @blk.scalar
                        def _(s, jt=jt):
                            s.wait_ge(sem_mm.h, sem_mm.n)
                            s.activation(out=etile[:, jt * 512:(jt + 1) * 512],
                                         in_=ps_sim[:], func=AF.Exp,
                                         bias=negbq_col[:],
                                         scale=0.125).then_inc(sem_a.h, 1)
                            sem_a.n += 1

                        if jt >= 4 * ich:
                            @blk.vector
                            def _(v, jt=jt, ich=ich):
                                v.wait_ge(sem_a.h, sem_a.n)
                                v.tensor_mul(out=etile[:, jt * 512:(jt + 1) * 512],
                                             in0=etile[:, jt * 512:(jt + 1) * 512],
                                             in1=dmaskb[jt - 4 * ich][:]).then_inc(sem_v.h, 1)
                                sem_v.n += 1

                        @blk.tensor
                        def _(t, jt=jt, njt=njt):
                            t.wait_ge(sem_a.h, sem_a.n)
                            t.wait_ge(sem_v.h, sem_v.n)
                            ins = t.matmul(ps_row[:], ones_col_b[:],
                                           etile[:, jt * 512:(jt + 1) * 512],
                                           start=(jt == 0), stop=(jt == njt - 1))
                            if jt == njt - 1:
                                ins.then_inc(sem_mm.h, 1)
                                sem_mm.n += 1

                    @blk.vector
                    def _(v):
                        v.wait_ge(sem_mm.h, sem_mm.n)
                        v.tensor_copy(out=den_sb[:], in_=ps_row[:]).then_inc(sem_v.h, 1)
                        sem_v.n += 1

                    for itl in range(4):
                        ti = ich * 4 + itl
                        @blk.tensor
                        def _(t, itl=itl):
                            t.wait_ge(sem_v.h, sem_v.n)
                            t.matmul(ps_col[:], den_sb[:, itl * 128:(itl + 1) * 128],
                                     ones_col[0:1, :], start=True,
                                     stop=True).then_inc(sem_mm.h, 1)
                            sem_mm.n += 1

                        @blk.vector
                        def _(v):
                            v.wait_ge(sem_mm.h, sem_mm.n)
                            v.reciprocal(out=recip_col[:], in_=ps_col[:]).then_inc(sem_v.h, 1)
                            sem_v.n += 1

                        @blk.tensor
                        def _(t, ti=ti, itl=itl, vv=sem_x.n):
                            t.wait_ge(sem_v.h, sem_v.n)
                            t.wait_ge(sem_x.h, vv)
                            for jt in range(ti + 1):
                                ins = t.matmul(ps_ctx[:],
                                               etile[:, jt * 512 + itl * 128:
                                                     jt * 512 + (itl + 1) * 128],
                                               v_h[:, jt * 64:(jt + 1) * 64],
                                               start=(jt == 0), stop=(jt == ti))
                            ins.then_inc(sem_mm.h, 1)
                            sem_mm.n += 1

                        @blk.vector
                        def _(v, ti=ti, h=h):
                            v.wait_ge(sem_mm.h, sem_mm.n)
                            v.tensor_scalar(out=ctx_tm[:, ti * 1024 + 64 * h:
                                                       ti * 1024 + 64 * h + 64],
                                            in0=ps_ctx[:], scalar1=recip_col[:],
                                            scalar2=None, op0=ALU.mult).then_inc(sem_v.h, 1)
                            sem_v.n += 1

            for tc in range(4):
                for ch2 in range(NCH):
                    for tis in range(4):
                        ti = tc * 4 + tis
                        @blk.tensor
                        def _(t, ti=ti, ch2=ch2):
                            t.wait_ge(sem_v.h, sem_v.n)
                            t.transpose(ps_tr[:],
                                        ctx_tm[:, ti * 1024 + ch2 * 128:
                                               ti * 1024 + (ch2 + 1) * 128],
                                        identb[:]).then_inc(sem_mm.h, 1)
                            sem_mm.n += 1

                        @blk.vector
                        def _(v, ch2=ch2, tis=tis):
                            v.wait_ge(sem_mm.h, sem_mm.n)
                            v.tensor_copy(out=cfm[ch2][:, tis * 128:(tis + 1) * 128],
                                          in_=ps_tr[:]).then_inc(sem_v.h, 1)
                            sem_v.n += 1

                for og in range(2):
                    wslice_load(blk, pbf, og * 512, 512, sem_mm.n)
                    for ol in range(4):
                        och = og * 4 + ol
                        @blk.tensor
                        def _(t, ol=ol, ww=sem_w.n):
                            t.wait_ge(sem_w.h, ww)
                            t.wait_ge(sem_v.h, sem_v.n)
                            t.wait_ge(sem_a.h, sem_a.n)
                            for dch in range(NCH):
                                ins = t.matmul(ps_main[:], wsl[dch][:, ol * 128:(ol + 1) * 128],
                                               cfm[dch][:, 0:512],
                                               start=(dch == 0), stop=(dch == NCH - 1))
                            ins.then_inc(sem_mm.h, 1)
                            sem_mm.n += 1

                        @blk.scalar
                        def _(s, och=och):
                            s.wait_ge(sem_mm.h, sem_mm.n)
                            s.wait_ge(sem_o.h, sem_o.n)
                            s.activation(out=dtile[:], in_=ps_main[:], func=AF.Identity,
                                         bias=projb_sb[och][:], scale=1.0).then_inc(sem_a.h, 1)
                            sem_a.n += 1

                        @blk.gpsimd
                        def _(g, och=och, tc=tc, t0=t0, first=not accum_started):
                            g.wait_ge(sem_a.h, sem_a.n)
                            if first:
                                g.wait_ge(sem_o.h, 16 * NCH)
                            g.dma_start(out=xout[och][:, t0 + tc * 512:t0 + (tc + 1) * 512],
                                        in_=dtile[:], accum_op=ALU.add).then_inc(sem_o.h, 16)
                            sem_o.n += 16
                        accum_started = True

        @blk.sync
        def _(sync):
            sync.wait_ge(sem_o.h, sem_o.n)

    p.es.close()
    return p.nc

def build_l0_ffn():
    p = Prog("l0_ffn")
    nc = p.nc
    x_hbm = p.inp("x_fm", [NCH, 128, T_LOC])
    W_r = p.inp("L0_ffn_W_r", [D, D])
    W_k = p.inp("L0_ffn_W_k", [D, D])
    W_v = p.inp("L0_ffn_W_v", [D, D])
    mu_r = p.inp("L0_ffn_mu_r", [D])
    mu_k = p.inp("L0_ffn_mu_k", [D])
    c_onescol = p.inp("C_ones_col", [128, 1])
    c_onesrow = p.inp("C_ones_row", [1, 128])
    xout = p.out("xout_fm", [NCH, 128, T_LOC])

    wrbf = p.dram("wrbf", [D, D], BF16)
    wkbf = p.dram("wkbf", [D, D], BF16)
    wvbf = p.dram("wvbf", [D, D], BF16)

    ps_main = p.ps("ps_main", [128, 512])
    ps_aux = p.ps("ps_aux", [128, 512])
    ps_row = p.ps("ps_row", [1, 512])
    ps_row2 = p.ps("ps_row2", [1, 512])

    arena = p.sb("arena", [128, 16 * 1024], F32)
    arena_b = arena[:].bitcast(BF16)
    x_ch = [arena[:, c * 2048:(c + 1) * 2048] for c in range(NCH)]
    xr = [arena_b[:, c * 2048:(c + 1) * 2048] for c in range(NCH)]
    xk = [arena_b[:, 16384 + c * 2048:16384 + (c + 1) * 2048] for c in range(NCH)]
    lnx = [p.sb(f"lnx{c}", [128, S], BF16) for c in range(NCH)]
    r_sb = [p.sb(f"r{c}", [128, S], BF16) for c in range(NCH)]
    kact = [p.sb(f"ka{c}", [128, S], BF16) for c in range(NCH)]
    wsl = [p.sb(f"wsl{c}", [128, 512], BF16) for c in range(NCH)]
    ones_col = p.sb("ones_col", [128, 1], F32)
    ones_row = p.sb("ones_row", [1, 128], F32)
    mur_sb = [p.sb(f"mur{c}", [128, 1], F32) for c in range(NCH)]
    muk_sb = [p.sb(f"muk{c}", [128, 1], F32) for c in range(NCH)]
    ktmp = p.sb("ktmp", [128, 512], BF16)
    tshift = p.sb("tshift", [128, 512], BF16)
    dtile = p.sb("dtile", [128, 512], F32)
    eps_col = p.sb("eps_col", [128, 1], F32)

    sem_x = p.sem("sx")
    sem_g = p.sem("sg")
    sem_w = p.sem("sw")
    sem_mm = p.sem("smm")
    sem_v = p.sem("sv")
    sem_a = p.sem("sa")
    sem_o = p.sem("so")
    sems = {"mm": sem_mm, "v": sem_v, "a": sem_a}

    with nc.Block() as blk:
        @blk.gpsimd
        def _(g):
            for c in range(NCH):
                g.dma_start(out=mur_sb[c][:],
                            in_=mu_r[c * 128:(c + 1) * 128].unsqueeze(1)).then_inc(sem_g.h, 16)
                g.dma_start(out=muk_sb[c][:],
                            in_=mu_k[c * 128:(c + 1) * 128].unsqueeze(1)).then_inc(sem_g.h, 16)
                sem_g.n += 32
            g.dma_start(out=wrbf, in_=W_r).then_inc(sem_g.h, 16)
            g.dma_start(out=wkbf, in_=W_k).then_inc(sem_g.h, 16)
            g.dma_start(out=wvbf, in_=W_v).then_inc(sem_g.h, 16)
            sem_g.n += 48

        @blk.sync
        def _(sync):
            sync.dma_start(out=ones_col[:], in_=c_onescol).then_inc(sem_x.h, 16)
            sync.dma_start(out=ones_row[:], in_=c_onesrow).then_inc(sem_x.h, 16)
            sem_x.n += 32
            for c in range(NCH):
                sync.dma_start(out=xout[c], in_=x_hbm[c]).then_inc(sem_o.h, 16)
                sem_o.n += 16

        @blk.vector
        def _(v):
            v.memset(eps_col[:], 1e-5).then_inc(sem_v.h, 1)
            sem_v.n += 1

        for eng in ("vector", "scalar", "tensor"):
            def mk(eng=eng):
                def f(e):
                    e.wait_ge(sem_x.h, sem_x.n)
                    e.wait_ge(sem_g.h, sem_g.n)
                    e.wait_ge(sem_v.h, sem_v.n)
                return f
            getattr(blk, eng)(mk())

        cst = {"ones_col": ones_col, "ones_row": ones_row, "eps_col": eps_col}
        psb = {"row": ps_row, "row2": ps_row2, "main": ps_main, "aux": ps_aux}
        wcast_done = sem_g.n

        def wslice_load(src_bf, o0, width, mm_guard):
            @blk.sync
            def _(sync):
                sync.wait_ge(sem_mm.h, mm_guard)
                sync.wait_ge(sem_g.h, wcast_done)
                for dch in range(NCH):
                    sync.dma_start(out=wsl[dch][:, 0:width],
                                   in_=src_bf[o0:o0 + width, dch * 128:(dch + 1) * 128],
                                   transpose=True).then_inc(sem_w.h, 16)
                    sem_w.n += 16

        accum_started = False
        for b in range(B_LOC):
            t0 = b * S

            @blk.sync
            def _(sync, t0=t0, guard=sem_mm.n):
                sync.wait_ge(sem_mm.h, guard)
                for c in range(NCH):
                    sync.dma_start(out=x_ch[c], in_=x_hbm[c][:, t0:t0 + S]
                                   ).then_inc(sem_x.h, 16)
                    sem_x.n += 16

            for eng in ("vector", "scalar", "tensor"):
                def mk(eng=eng):
                    def f(e):
                        e.wait_ge(sem_x.h, sem_x.n)
                    return f
                getattr(blk, eng)(mk())

            emit_layernorm(p, blk, x_ch, lnx, sems, cst, psb, f"l0f_b{b}",
                           t_lo=0, t_hi=S)

            for dst, mu in ((xr, mur_sb), (xk, muk_sb)):
                for ch in range(NCH):
                    for tc in range(4):
                        @blk.vector
                        def _(v, dst=dst, mu=mu, ch=ch, tc=tc):
                            v.wait_ge(sem_v.h, sem_v.n)
                            c0 = tc * 512
                            if tc == 0:
                                v.tensor_sub(out=tshift[:, 1:512],
                                             in0=lnx[ch][:, 1:512], in1=lnx[ch][:, 0:511])
                                v.scalar_tensor_tensor(
                                    out=dst[ch][:, 1:512], in0=tshift[:, 1:512],
                                    scalar=mu[ch][:], in1=lnx[ch][:, 0:511],
                                    op0=ALU.mult, op1=ALU.add)
                                v.tensor_scalar(out=dst[ch][:, 0:1], in0=lnx[ch][:, 0:1],
                                                scalar1=mu[ch][:], scalar2=None,
                                                op0=ALU.mult).then_inc(sem_v.h, 1)
                                sem_v.n += 1
                            else:
                                v.tensor_sub(out=tshift[:], in0=lnx[ch][:, c0:c0 + 512],
                                             in1=lnx[ch][:, c0 - 1:c0 + 511])
                                v.scalar_tensor_tensor(
                                    out=dst[ch][:, c0:c0 + 512], in0=tshift[:],
                                    scalar=mu[ch][:], in1=lnx[ch][:, c0 - 1:c0 + 511],
                                    op0=ALU.mult, op1=ALU.add).then_inc(sem_v.h, 1)
                                sem_v.n += 1

            for (src, wsrc, dst_sb, act) in ((xr, wrbf, r_sb, "sig"),
                                             (xk, wkbf, kact, "relu2")):
                for og in range(2):
                    wslice_load(wsrc, og * 512, 512, sem_mm.n)
                    for ol in range(4):
                        och = og * 4 + ol
                        for tc in range(4):
                            @blk.tensor
                            def _(t, src=src, ol=ol, tc=tc, ww=sem_w.n):
                                t.wait_ge(sem_w.h, ww)
                                t.wait_ge(sem_a.h, sem_a.n)
                                t.wait_ge(sem_v.h, sem_v.n)
                                for dch in range(NCH):
                                    ins = t.matmul(ps_main[:],
                                                   wsl[dch][:, ol * 128:(ol + 1) * 128],
                                                   src[dch][:, tc * 512:(tc + 1) * 512],
                                                   start=(dch == 0), stop=(dch == NCH - 1))
                                ins.then_inc(sem_mm.h, 1)
                                sem_mm.n += 1

                            if act == "sig":
                                @blk.scalar
                                def _(s, och=och, tc=tc, dst_sb=dst_sb):
                                    s.wait_ge(sem_mm.h, sem_mm.n)
                                    s.activation(out=dst_sb[och][:, tc * 512:(tc + 1) * 512],
                                                 in_=ps_main[:],
                                                 func=AF.Sigmoid).then_inc(sem_a.h, 1)
                                    sem_a.n += 1
                            else:
                                @blk.scalar
                                def _(s, och=och, tc=tc):
                                    s.wait_ge(sem_mm.h, sem_mm.n)
                                    s.wait_ge(sem_v.h, sem_v.n)
                                    s.activation(out=ktmp[:], in_=ps_main[:],
                                                 func=AF.Relu).then_inc(sem_a.h, 1)
                                    sem_a.n += 1

                                @blk.vector
                                def _(v, och=och, tc=tc, dst_sb=dst_sb):
                                    v.wait_ge(sem_a.h, sem_a.n)
                                    v.tensor_mul(out=dst_sb[och][:, tc * 512:(tc + 1) * 512],
                                                 in0=ktmp[:], in1=ktmp[:]).then_inc(sem_v.h, 1)
                                    sem_v.n += 1

            for og in range(2):
                wslice_load(wvbf, og * 512, 512, sem_mm.n)
                for ol in range(4):
                    och = og * 4 + ol
                    for tc in range(4):
                        @blk.tensor
                        def _(t, ol=ol, tc=tc, ww=sem_w.n):
                            t.wait_ge(sem_w.h, ww)
                            t.wait_ge(sem_v.h, sem_v.n)
                            for dch in range(NCH):
                                ins = t.matmul(ps_main[:],
                                               wsl[dch][:, ol * 128:(ol + 1) * 128],
                                               kact[dch][:, tc * 512:(tc + 1) * 512],
                                               start=(dch == 0), stop=(dch == NCH - 1))
                            ins.then_inc(sem_mm.h, 1)
                            sem_mm.n += 1

                        @blk.vector
                        def _(v, och=och, tc=tc):
                            v.wait_ge(sem_mm.h, sem_mm.n)
                            v.wait_ge(sem_o.h, sem_o.n)
                            v.tensor_mul(out=dtile[:], in0=ps_main[:],
                                         in1=r_sb[och][:, tc * 512:(tc + 1) * 512]
                                         ).then_inc(sem_v.h, 1)
                            sem_v.n += 1

                        @blk.gpsimd
                        def _(g, och=och, tc=tc, t0=t0, first=not accum_started):
                            g.wait_ge(sem_v.h, sem_v.n)
                            if first:
                                g.wait_ge(sem_o.h, 16 * NCH)
                            g.dma_start(out=xout[och][:, t0 + tc * 512:t0 + (tc + 1) * 512],
                                        in_=dtile[:], accum_op=ALU.add).then_inc(sem_o.h, 16)
                            sem_o.n += 16
                        accum_started = True

        @blk.sync
        def _(sync):
            sync.wait_ge(sem_o.h, sem_o.n)

    p.es.close()
    return p.nc


def build_l1_attn():
    p = Prog("l1_attn")
    nc = p.nc
    x_hbm = p.inp("x_fm", [NCH, 128, T_LOC])
    wqk = p.inp("L1_attn_wqk", [D, 128])
    bqk = p.inp("L1_attn_bqk", [128])
    gamma = p.inp("L1_attn_gamma", [2, 128])
    beta = p.inp("L1_attn_beta", [2, 128])
    wv = p.inp("L1_attn_wv", [D, D])
    bv = p.inp("L1_attn_bv", [D])
    relb = p.inp("L1_attn_rel_bias", [32])
    c_oht = p.inp("C_oht", [32, 2048])
    c_rev = p.inp("C_rev", [128, 128])
    c_onescol = p.inp("C_ones_col", [128, 1])
    c_onesrow = p.inp("C_ones_row", [1, 128])
    c_dmask = [p.inp(f"C_dmask{k}", [128, 512]) for k in range(4)]
    xout = p.out("xout_fm", [NCH, 128, T_LOC])

    vbuf = p.dram("vbuf", [16, 128, D], BF16)
    fpad_hi = p.dram("fpad_hi", [4608], F32)
    fpad_lo = p.dram("fpad_lo", [4608], F32)

    ps_main = p.ps("ps_main", [128, 512])
    ps_aux = p.ps("ps_aux", [128, 512])
    ps_sim = p.ps("ps_sim", [128, 512])
    ps_pv = p.ps("ps_pv", [128, 512])
    ps_row = p.ps("ps_row", [1, 512])
    ps_row2 = p.ps("ps_row2", [1, 512])
    ps_col = p.ps("ps_col", [128, 1])

    arena = p.sb("arena", [128, 16 * 1024], F32)
    x_ch = [arena[:, c * 2048:(c + 1) * 2048] for c in range(NCH)]
    lnx = [p.sb(f"lnx{c}", [128, S], BF16) for c in range(NCH)]
    qfm = p.sb("qfm", [128, S], BF16)
    kfm = p.sb("kfm", [128, S], BF16)
    etile = p.sb("etile", [128, 16 * 512], BF16)
    vsl = p.sb("vsl", [128, 16 * 128], BF16)
    wqk_sb = [p.sb(f"wqk{c}", [128, 128], BF16) for c in range(NCH)]
    wv_sb = [p.sb(f"wv{c}", [128, 1024], BF16) for c in range(NCH)]
    oht_sb = p.sb("oht", [32, 2048], BF16)
    relb_sb = p.sb("relb", [32, 1], F32)
    relb_hi = p.sb("relb_hi", [32, 1], BF16)
    relb_hi32 = p.sb("relb_hi32", [32, 1], F32)
    relb_lo32 = p.sb("relb_lo32", [32, 1], F32)
    relb_lo = p.sb("relb_lo", [32, 1], BF16)
    fcol = p.sb("fcol", [128, 16], F32)
    fhib = p.sb("fhib", [128, 16], BF16)
    fhi32 = p.sb("fhi32", [128, 16], F32)
    flo32 = p.sb("flo32", [128, 16], F32)
    ztile = p.sb("ztile", [128, 36], BF16)
    ztile32 = p.sb("ztile32", [128, 36], F32)
    fwin = p.sb("fwin", [128, 512], BF16)
    fwin_lo = p.sb("fwin_lo", [128, 512], BF16)
    revb = p.sb("revb", [128, 128], BF16)
    ones_col = p.sb("ones_col", [128, 1], F32)
    ones_row = p.sb("ones_row", [1, 128], F32)
    ones_row_b = p.sb("ones_row_b", [1, 128], BF16)
    dmaskb = [p.sb(f"dmask{k}", [128, 512], BF16) for k in range(4)]
    bqk_col = p.sb("bqk_col", [128, 1], F32)
    g0_col = p.sb("g0_col", [128, 1], F32)
    g1_col = p.sb("g1_col", [128, 1], F32)
    b0_col = p.sb("b0_col", [128, 1], F32)
    b1_col = p.sb("b1_col", [128, 1], F32)
    lap_col = p.sb("lap_col", [128, 1], F32)
    bv_row = p.sb("bv_row", [1, D], BF16)
    vbias_t = p.sb("vbias_t", [128, 512], BF16)
    qtmp = p.sb("qtmp", [128, 512], BF16)
    vstage = p.sb("vstage", [128, 512], BF16)
    vtmpf = p.sb("vtmpf", [128, 512], F32)
    dtile = p.sb("dtile", [128, 512], F32)
    eps_col = p.sb("eps_col", [128, 1], F32)

    sem_x = p.sem("sx")
    sem_g = p.sem("sg")
    sem_mm = p.sem("smm")
    sem_v = p.sem("sv")
    sem_a = p.sem("sa")
    sem_o = p.sem("so")
    sems = {"mm": sem_mm, "v": sem_v, "a": sem_a}

    LAP_SCALE = 1.0 / math.sqrt(0.5 * math.pi)
    LAP_BIAS = -math.sqrt(0.5) * LAP_SCALE

    with nc.Block() as blk:
        @blk.gpsimd
        def _(g):
            for c in range(NCH):
                g.dma_start(out=wqk_sb[c][:], in_=wqk[c * 128:(c + 1) * 128, :]
                            ).then_inc(sem_g.h, 16)
                g.dma_start(out=wv_sb[c][:], in_=wv[c * 128:(c + 1) * 128, :]
                            ).then_inc(sem_g.h, 16)
                sem_g.n += 32
            g.dma_start(out=oht_sb[:], in_=c_oht).then_inc(sem_g.h, 16)
            g.dma_start(out=relb_sb[:], in_=relb.unsqueeze(1)).then_inc(sem_g.h, 16)
            g.dma_start(out=bqk_col[:], in_=bqk.unsqueeze(1)).then_inc(sem_g.h, 16)
            g.dma_start(out=g0_col[:], in_=gamma[0].unsqueeze(1)).then_inc(sem_g.h, 16)
            g.dma_start(out=g1_col[:], in_=gamma[1].unsqueeze(1)).then_inc(sem_g.h, 16)
            g.dma_start(out=b0_col[:], in_=beta[0].unsqueeze(1)).then_inc(sem_g.h, 16)
            g.dma_start(out=b1_col[:], in_=beta[1].unsqueeze(1)).then_inc(sem_g.h, 16)
            g.dma_start(out=bv_row[:], in_=bv.unsqueeze(0)).then_inc(sem_g.h, 16)
            sem_g.n += 8 * 16
            for k in range(4):
                g.dma_start(out=dmaskb[k][:], in_=c_dmask[k]).then_inc(sem_g.h, 16)
                sem_g.n += 16

        @blk.sync
        def _(sync):
            sync.dma_start(out=ones_col[:], in_=c_onescol).then_inc(sem_x.h, 16)
            sync.dma_start(out=ones_row[:], in_=c_onesrow).then_inc(sem_x.h, 16)
            sem_x.n += 32
            for c in range(NCH):
                sync.dma_start(out=xout[c], in_=x_hbm[c]).then_inc(sem_o.h, 16)
                sem_o.n += 16

        @blk.gpsimd
        def _(g):
            g.dma_start(out=ones_row_b[:], in_=c_onesrow).then_inc(sem_g.h, 16)
            g.dma_start(out=revb[:], in_=c_rev).then_inc(sem_g.h, 16)
            sem_g.n += 32

        @blk.vector
        def _(v):
            v.wait_ge(sem_g.h, sem_g.n)
            v.memset(eps_col[:], 1e-5)
            v.memset(lap_col[:], LAP_BIAS)
            v.memset(ztile[:], 0.0)
            v.memset(ztile32[:], 0.0)
            v.tensor_scalar(out=g0_col[:], in0=g0_col[:], scalar1=1.0 / S,
                            scalar2=None, op0=ALU.mult)
            v.tensor_scalar(out=b0_col[:], in0=b0_col[:], scalar1=1.0 / S,
                            scalar2=None, op0=ALU.mult).then_inc(sem_v.h, 1)
            sem_v.n += 1

        for eng in ("vector", "scalar", "tensor"):
            def mk(eng=eng):
                def f(e):
                    e.wait_ge(sem_x.h, sem_x.n)
                    e.wait_ge(sem_g.h, sem_g.n)
                    e.wait_ge(sem_v.h, sem_v.n)
                return f
            getattr(blk, eng)(mk())

        @blk.sync
        def _(sync):
            sync.wait_ge(sem_v.h, sem_v.n)
            sync.dma_start(out=fpad_hi.rearrange("(a b) -> a b", a=128),
                           in_=ztile32[:]).then_inc(sem_x.h, 16)
            sync.dma_start(out=fpad_lo.rearrange("(a b) -> a b", a=128),
                           in_=ztile32[:]).then_inc(sem_x.h, 16)
            sem_x.n += 32

        @blk.vector
        def _(v):
            v.wait_ge(sem_g.h, sem_g.n)
            v.tensor_copy(out=relb_hi[:], in_=relb_sb[:]).then_inc(sem_v.h, 1)
            sem_v.n += 1
            v.wait_ge(sem_v.h, sem_v.n)
            v.tensor_copy(out=relb_hi32[:], in_=relb_hi[:]).then_inc(sem_v.h, 1)
            sem_v.n += 1
            v.wait_ge(sem_v.h, sem_v.n)
            v.tensor_sub(out=relb_lo32[:], in0=relb_sb[:], in1=relb_hi32[:]).then_inc(sem_v.h, 1)
            sem_v.n += 1
            v.wait_ge(sem_v.h, sem_v.n)
            v.tensor_copy(out=relb_lo[:], in_=relb_lo32[:]).then_inc(sem_v.h, 1)
            sem_v.n += 1

        for c in range(16):
            @blk.tensor
            def _(t, c=c):
                t.wait_ge(sem_v.h, sem_v.n)
                t.matmul(ps_col[:], oht_sb[:, c * 128:(c + 1) * 128], relb_hi[:],
                         start=True, stop=False)
                t.matmul(ps_col[:], oht_sb[:, c * 128:(c + 1) * 128], relb_lo[:],
                         start=False, stop=True).then_inc(sem_mm.h, 1)
                sem_mm.n += 1

            @blk.vector
            def _(v, c=c):
                v.wait_ge(sem_mm.h, sem_mm.n)
                v.tensor_scalar(out=fcol[:, c:c + 1], in0=ps_col[:],
                                scalar1=math.sqrt(128.0), scalar2=None,
                                op0=ALU.mult).then_inc(sem_v.h, 1)
                sem_v.n += 1

        @blk.vector
        def _(v):
            v.wait_ge(sem_v.h, sem_v.n)
            v.tensor_copy(out=fhib[:], in_=fcol[:]).then_inc(sem_v.h, 1)
            sem_v.n += 1
            v.wait_ge(sem_v.h, sem_v.n)
            v.tensor_copy(out=fhi32[:], in_=fhib[:]).then_inc(sem_v.h, 1)
            sem_v.n += 1
            v.wait_ge(sem_v.h, sem_v.n)
            v.tensor_sub(out=flo32[:], in0=fcol[:], in1=fhi32[:]).then_inc(sem_v.h, 1)
            sem_v.n += 1

        @blk.gpsimd
        def _(g):
            g.wait_ge(sem_v.h, sem_v.n)
            g.wait_ge(sem_x.h, sem_x.n)
            for c in range(16):
                g.dma_start(out=fpad_hi[2048 + c * 128:2048 + (c + 1) * 128].unsqueeze(1),
                            in_=fhi32[:, c:c + 1]).then_inc(sem_g.h, 16)
                g.dma_start(out=fpad_lo[2048 + c * 128:2048 + (c + 1) * 128].unsqueeze(1),
                            in_=flo32[:, c:c + 1]).then_inc(sem_g.h, 16)
                sem_g.n += 32

        cst = {"ones_col": ones_col, "ones_row": ones_row, "eps_col": eps_col}
        psb = {"row": ps_row, "row2": ps_row2, "main": ps_main, "aux": ps_aux}

        def dump(tag, aps):
            if DBG != tag:
                return False
            outs = [p.out(f"dbg{i}", list(a.shape), a.dtype)
                    for i, a in enumerate(aps)]
            @blk.sync
            def _(sync):
                for sm in (sem_v, sem_a, sem_mm, sem_g, sem_o, sem_x):
                    sync.wait_ge(sm.h, sm.n)
                for o, a in zip(outs, aps):
                    sync.dma_start(out=o, in_=a).then_inc(sem_x.h, 16)
                    sem_x.n += 16
                sync.wait_ge(sem_x.h, sem_x.n)
            return True

        dumped = dump("l1f", [fcol[:], relb_sb[:], relb_hi[:], relb_lo[:]])

        accum_started = False
        for b in range(B_LOC if not dumped else 0):
            t0 = b * S

            @blk.sync
            def _(sync, t0=t0, guard=sem_mm.n):
                sync.wait_ge(sem_mm.h, guard)
                for c in range(NCH):
                    sync.dma_start(out=x_ch[c], in_=x_hbm[c][:, t0:t0 + S]
                                   ).then_inc(sem_x.h, 16)
                    sem_x.n += 16

            for eng in ("vector", "scalar", "tensor"):
                def mk(eng=eng):
                    def f(e):
                        e.wait_ge(sem_x.h, sem_x.n)
                    return f
                getattr(blk, eng)(mk())

            emit_layernorm(p, blk, x_ch, lnx, sems, cst, psb, f"l1a_b{b}",
                           t_lo=0, t_hi=S)

            for tc in range(4):
                @blk.tensor
                def _(t, tc=tc):
                    t.wait_ge(sem_a.h, sem_a.n)
                    t.wait_ge(sem_v.h, sem_v.n)
                    for dch in range(NCH):
                        ins = t.matmul(ps_main[:], wqk_sb[dch][:],
                                       lnx[dch][:, tc * 512:(tc + 1) * 512],
                                       start=(dch == 0), stop=(dch == NCH - 1))
                    ins.then_inc(sem_mm.h, 1)
                    sem_mm.n += 1

                @blk.scalar
                def _(s, tc=tc):
                    s.wait_ge(sem_mm.h, sem_mm.n)
                    s.wait_ge(sem_v.h, sem_v.n)
                    s.activation(out=qtmp[:], in_=ps_main[:], func=AF.Silu,
                                 bias=bqk_col[:], scale=1.0).then_inc(sem_a.h, 1)
                    sem_a.n += 1

                @blk.vector
                def _(v, tc=tc):
                    v.wait_ge(sem_a.h, sem_a.n)
                    v.tensor_scalar(out=qfm[:, tc * 512:(tc + 1) * 512], in0=qtmp[:],
                                    scalar1=g0_col[:], scalar2=b0_col[:],
                                    op0=ALU.mult, op1=ALU.add)
                    v.tensor_scalar(out=kfm[:, tc * 512:(tc + 1) * 512], in0=qtmp[:],
                                    scalar1=g1_col[:], scalar2=b1_col[:],
                                    op0=ALU.mult, op1=ALU.add).then_inc(sem_v.h, 1)
                    sem_v.n += 1

            for och in range(2):
                @blk.tensor
                def _(t, och=och):
                    t.wait_ge(sem_a.h, sem_a.n)
                    t.matmul(ps_aux[:], ones_row_b[:],
                             bv_row[:, och * 512:(och + 1) * 512],
                             start=True, stop=True).then_inc(sem_mm.h, 1)
                    sem_mm.n += 1

                @blk.scalar
                def _(s):
                    s.wait_ge(sem_mm.h, sem_mm.n)
                    s.activation(out=vbias_t[:], in_=ps_aux[:],
                                 func=AF.Identity, scale=1.0).then_inc(sem_a.h, 1)
                    sem_a.n += 1

                for tt in range(16):
                    @blk.tensor
                    def _(t, tt=tt, och=och):
                        t.wait_ge(sem_a.h, sem_a.n)
                        t.wait_ge(sem_v.h, sem_v.n)
                        for dch in range(NCH):
                            ins = t.matmul(ps_aux[:], lnx[dch][:, tt * 128:(tt + 1) * 128],
                                           wv_sb[dch][:, och * 512:(och + 1) * 512],
                                           start=(dch == 0), stop=(dch == NCH - 1))
                        ins.then_inc(sem_mm.h, 1)
                        sem_mm.n += 1

                    @blk.vector
                    def _(v):
                        v.wait_ge(sem_mm.h, sem_mm.n)
                        v.wait_ge(sem_a.h, sem_a.n)
                        v.tensor_add(out=vtmpf[:], in0=ps_aux[:],
                                     in1=vbias_t[:]).then_inc(sem_v.h, 1)
                        sem_v.n += 1

                    @blk.scalar
                    def _(s):
                        s.wait_ge(sem_v.h, sem_v.n)
                        s.wait_ge(sem_g.h, sem_g.n)
                        s.activation(out=vstage[:], in_=vtmpf[:],
                                     func=AF.Silu).then_inc(sem_a.h, 1)
                        sem_a.n += 1

                    @blk.gpsimd
                    def _(g, tt=tt, och=och):
                        g.wait_ge(sem_a.h, sem_a.n)
                        g.dma_start(out=vbuf[tt][:, och * 512:(och + 1) * 512],
                                    in_=vstage[:]).then_inc(sem_g.h, 16)
                        sem_g.n += 16

            for ich in range(4):
                i0 = ich * 512
                njt = 4 * ich + 4
                for jt in range(njt):
                    @blk.gpsimd
                    def _(g, jt=jt, i0=i0, guard=sem_a.n):
                        g.wait_ge(sem_a.h, guard)
                        d0 = 2048 + i0 - jt * 128 - 127
                        g.dma_start(out=fwin[:],
                                    in_=_win_ap(fpad_hi, d0, 512)).then_inc(sem_g.h, 16)
                        g.dma_start(out=fwin_lo[:],
                                    in_=_win_ap(fpad_lo, d0, 512)).then_inc(sem_g.h, 16)
                        sem_g.n += 32

                    @blk.tensor
                    def _(t, jt=jt, i0=i0, fw=sem_g.n):
                        t.wait_ge(sem_v.h, sem_v.n)
                        t.wait_ge(sem_a.h, sem_a.n)
                        t.wait_ge(sem_g.h, fw)
                        t.matmul(ps_sim[:], kfm[:, jt * 128:(jt + 1) * 128],
                                 qfm[:, i0:i0 + 512], start=True, stop=False)
                        t.matmul(ps_sim[:], revb[:], fwin[:], start=False, stop=False)
                        t.matmul(ps_sim[:], revb[:], fwin_lo[:],
                                 start=False, stop=True).then_inc(sem_mm.h, 1)
                        sem_mm.n += 1

                    @blk.scalar
                    def _(s, jt=jt):
                        s.wait_ge(sem_mm.h, sem_mm.n)
                        s.activation(out=etile[:, jt * 512:(jt + 1) * 512],
                                     in_=ps_sim[:], func=AF.Erf,
                                     bias=lap_col[:],
                                     scale=LAP_SCALE).then_inc(sem_a.h, 1)
                        sem_a.n += 1

                    if jt >= 4 * ich:
                        @blk.vector
                        def _(v, jt=jt, ich=ich):
                            v.wait_ge(sem_a.h, sem_a.n)
                            v.tensor_scalar(out=etile[:, jt * 512:(jt + 1) * 512],
                                            in0=etile[:, jt * 512:(jt + 1) * 512],
                                            scalar1=0.5, scalar2=0.5,
                                            op0=ALU.mult, op1=ALU.add)
                            v.tensor_mul(out=etile[:, jt * 512:(jt + 1) * 512],
                                         in0=etile[:, jt * 512:(jt + 1) * 512],
                                         in1=dmaskb[jt - 4 * ich][:]).then_inc(sem_v.h, 1)
                            sem_v.n += 1
                    else:
                        @blk.vector
                        def _(v, jt=jt):
                            v.wait_ge(sem_a.h, sem_a.n)
                            v.tensor_scalar(out=etile[:, jt * 512:(jt + 1) * 512],
                                            in0=etile[:, jt * 512:(jt + 1) * 512],
                                            scalar1=0.5, scalar2=0.5,
                                            op0=ALU.mult, op1=ALU.add).then_inc(sem_v.h, 1)
                            sem_v.n += 1

                for dch in range(NCH):
                    @blk.sync
                    def _(sync, dch=dch, guard=sem_mm.n):
                        sync.wait_ge(sem_mm.h, guard)
                        sync.wait_ge(sem_g.h, sem_g.n)
                        sync.dma_start(out=vsl[:].rearrange("p (a c) -> p a c", c=128),
                                       in_=vbuf[:, :, dch * 128:(dch + 1) * 128].transpose(
                                           [1, 0, 2])).then_inc(sem_x.h, 16)
                        sem_x.n += 16

                    @blk.tensor
                    def _(t, dch=dch, njt=njt, vv=sem_x.n):
                        t.wait_ge(sem_v.h, sem_v.n)
                        t.wait_ge(sem_x.h, vv)
                        t.wait_ge(sem_o.h, sem_o.n)
                        for jt in range(njt):
                            ins = t.matmul(ps_pv[:], vsl[:, jt * 128:(jt + 1) * 128],
                                           etile[:, jt * 512:(jt + 1) * 512],
                                           start=(jt == 0), stop=(jt == njt - 1))
                        ins.then_inc(sem_mm.h, 1)
                        sem_mm.n += 1

                    @blk.vector
                    def _(v):
                        v.wait_ge(sem_mm.h, sem_mm.n)
                        v.wait_ge(sem_o.h, sem_o.n)
                        v.tensor_copy(out=dtile[:], in_=ps_pv[:]).then_inc(sem_v.h, 1)
                        sem_v.n += 1

                    @blk.gpsimd
                    def _(g, dch=dch, i0=i0, t0=t0, first=not accum_started):
                        g.wait_ge(sem_v.h, sem_v.n)
                        if first:
                            g.wait_ge(sem_o.h, 16 * NCH)
                        g.dma_start(out=xout[dch][:, t0 + i0:t0 + i0 + 512],
                                    in_=dtile[:], accum_op=ALU.add).then_inc(sem_o.h, 16)
                        sem_o.n += 16
                    accum_started = True

        @blk.sync
        def _(sync):
            sync.wait_ge(sem_o.h, sem_o.n)

    p.es.close()
    return p.nc


def _win_ap(fpad, d0, width):
    from concourse.ap import AP as _AP
    return _AP(tensor=fpad.tensor, offset=d0, ap=[[1, 128], [1, width]])


def build_l2_ffn():
    p = Prog("l2_ffn")
    nc = p.nc
    x_hbm = p.inp("x_fm", [NCH, 128, T_LOC])
    wrt = p.inp("WrT", [16, D, D])
    wst = p.inp("WsT", [2, D, D])
    centT = p.inp("centT", [D, 16])
    ebias = p.inp("L2_ffn_expert_biases", [16])
    br = p.inp("L2_ffn_br", [16, D])
    bs = p.inp("L2_ffn_bs", [2, D])
    c_onescol = p.inp("C_ones_col", [128, 1])
    c_onesrow = p.inp("C_ones_row", [1, 128])
    c_ident = p.inp("C_ident", [128, 128])
    xout = p.out("xout_fm", [NCH, 128, T_LOC])

    wrtbf = p.dram("wrtbf", [16, D, D], BF16)

    ps_main = p.ps("ps_main", [128, 512])
    ps_aux = p.ps("ps_aux", [128, 512])
    ps_row = p.ps("ps_row", [1, 512])
    ps_row2 = p.ps("ps_row2", [1, 512])
    ps_r = p.ps("ps_r", [128, 16])
    ps_tr = p.ps("ps_tr", [128, 128])
    ps_gtr = p.ps("ps_gtr", [16, 128], BF16)

    arena = p.sb("arena", [128, 16 * 1024], F32)
    x_ch = [arena[:, c * 2048:(c + 1) * 2048] for c in range(NCH)]
    lnx = [p.sb(f"lnx{c}", [128, S], BF16) for c in range(NCH)]
    wst_sb = [p.sb(f"wst{c}", [128, 1024], BF16) for c in range(NCH)]
    acc8 = p.sb("acc8", [128, 8 * 1024], F32)
    wslE = [p.sb(f"wslE{c}", [128, 1024], BF16) for c in range(NCH)]
    centT_sb = [p.sb(f"cent{c}", [128, 16], BF16) for c in range(NCH)]
    eb_row = p.sb("eb_row", [1, 16], F32)
    eb_t = p.sb("eb_t", [128, 16], F32)
    brb = p.sb("brb", [16, D], BF16)
    bs0 = p.sb("bs0", [1, D], F32)
    bs1 = p.sb("bs1", [1, D], F32)
    bsrow = p.sb("bsrow", [1, D], F32)
    bs_t = p.sb("bs_t", [128, 1024], F32)
    g_sup = p.sb("g_sup", [128, 8 * 16], F32)
    gT_sb = [p.sb(f"gT{i}", [16, 128], BF16) for i in range(8)]
    gb_t = p.sb("gb_t", [128, 16], BF16)
    rt = p.sb("rt", [128, 16], F32)
    ex_t = p.sb("ex_t", [128, 16], F32)
    s_t = p.sb("s_t", [128, 16], F32)
    s2_t = p.sb("s2_t", [128, 16], F32)
    eq_t = p.sb("eq_t", [128, 16], F32)
    mask_t = p.sb("mask_t", [128, 16], F32)
    m1c = p.sb("m1c", [128, 1], F32)
    m2c = p.sb("m2c", [128, 1], F32)
    negm = p.sb("negm", [128, 1], F32)
    denc = p.sb("denc", [128, 1], F32)
    rdc = p.sb("rdc", [128, 1], F32)
    cntc = p.sb("cntc", [128, 1], F32)
    ge2c = p.sb("ge2c", [128, 1], F32)
    dmc = p.sb("dmc", [128, 1], F32)
    identb = p.sb("identb", [128, 128], BF16)
    identf = p.sb("identf", [128, 128], F32)
    ones_col = p.sb("ones_col", [128, 1], F32)
    ones_row = p.sb("ones_row", [1, 128], F32)
    eps_col = p.sb("eps_col", [128, 1], F32)

    sem_x = p.sem("sx")
    sem_g = p.sem("sg")
    sem_w = p.sem("sw")
    sem_mm = p.sem("smm")
    sem_v = p.sem("sv")
    sem_a = p.sem("sa")
    sem_o = p.sem("so")
    sems = {"mm": sem_mm, "v": sem_v, "a": sem_a}

    with nc.Block() as blk:
        @blk.gpsimd
        def _(g):
            for c in range(NCH):
                g.dma_start(out=centT_sb[c][:], in_=centT[c * 128:(c + 1) * 128, :]
                            ).then_inc(sem_g.h, 16)
                sem_g.n += 16
            g.dma_start(out=brb[:], in_=br).then_inc(sem_g.h, 16)
            g.dma_start(out=eb_row[:], in_=ebias.unsqueeze(0)).then_inc(sem_g.h, 16)
            g.dma_start(out=bs0[:], in_=bs[0].unsqueeze(0)).then_inc(sem_g.h, 16)
            g.dma_start(out=bs1[:], in_=bs[1].unsqueeze(0)).then_inc(sem_g.h, 16)
            g.dma_start(out=identb[:], in_=c_ident).then_inc(sem_g.h, 16)
            g.dma_start(out=identf[:], in_=c_ident).then_inc(sem_g.h, 16)
            sem_g.n += 6 * 16
            for e in range(16):
                g.dma_start(out=wrtbf[e], in_=wrt[e]).then_inc(sem_g.h, 16)
                sem_g.n += 16

        @blk.sync
        def _(sync):
            sync.dma_start(out=ones_col[:], in_=c_onescol).then_inc(sem_x.h, 16)
            sync.dma_start(out=ones_row[:], in_=c_onesrow).then_inc(sem_x.h, 16)
            sem_x.n += 32
            for c in range(NCH):
                sync.dma_start(out=xout[c], in_=x_hbm[c]).then_inc(sem_o.h, 16)
                sem_o.n += 16

        @blk.vector
        def _(v):
            v.memset(eps_col[:], 1e-5).then_inc(sem_v.h, 1)
            sem_v.n += 1

        for eng in ("vector", "scalar", "tensor"):
            def mk(eng=eng):
                def f(e):
                    e.wait_ge(sem_x.h, sem_x.n)
                    e.wait_ge(sem_g.h, sem_g.n)
                    e.wait_ge(sem_v.h, sem_v.n)
                return f
            getattr(blk, eng)(mk())

        wcast_done = sem_g.n

        @blk.vector
        def _(v):
            v.tensor_add(out=bsrow[:], in0=bs0[:], in1=bs1[:]).then_inc(sem_v.h, 1)
            sem_v.n += 1
            v.wait_ge(sem_v.h, sem_v.n)
            v.tensor_scalar(out=bsrow[:], in0=bsrow[:], scalar1=0.5, scalar2=None,
                            op0=ALU.mult).then_inc(sem_v.h, 1)
            sem_v.n += 1

        @blk.tensor
        def _(t):
            t.wait_ge(sem_v.h, sem_v.n)
            t.matmul(ps_r[:], ones_row[:], eb_row[:], start=True, stop=True
                     ).then_inc(sem_mm.h, 1)
            sem_mm.n += 1

        @blk.vector
        def _(v):
            v.wait_ge(sem_mm.h, sem_mm.n)
            v.tensor_copy(out=eb_t[:], in_=ps_r[:]).then_inc(sem_v.h, 1)
            sem_v.n += 1

        for hch in range(2):
            @blk.tensor
            def _(t, hch=hch):
                t.wait_ge(sem_v.h, sem_v.n)
                t.matmul(ps_main[:], ones_row[:], bsrow[:, hch * 512:(hch + 1) * 512],
                         start=True, stop=True).then_inc(sem_mm.h, 1)
                sem_mm.n += 1

            @blk.vector
            def _(v, hch=hch):
                v.wait_ge(sem_mm.h, sem_mm.n)
                v.tensor_copy(out=bs_t[:, hch * 512:(hch + 1) * 512],
                              in_=ps_main[:]).then_inc(sem_v.h, 1)
                sem_v.n += 1

        cst = {"ones_col": ones_col, "ones_row": ones_row, "eps_col": eps_col}
        psb = {"row": ps_row, "row2": ps_row2, "main": ps_main, "aux": ps_aux}

        accum_started = False
        for b in range(B_LOC):
            t0 = b * S

            @blk.sync
            def _(sync, t0=t0, guard=sem_mm.n):
                sync.wait_ge(sem_mm.h, guard)
                for c in range(NCH):
                    sync.dma_start(out=x_ch[c], in_=x_hbm[c][:, t0:t0 + S]
                                   ).then_inc(sem_x.h, 16)
                    sem_x.n += 16

            for eng in ("vector", "scalar", "tensor"):
                def mk(eng=eng):
                    def f(e):
                        e.wait_ge(sem_x.h, sem_x.n)
                    return f
                getattr(blk, eng)(mk())

            emit_layernorm(p, blk, x_ch, lnx, sems, cst, psb, f"l2f_b{b}",
                           t_lo=0, t_hi=S)

            for sup in range(2):
                @blk.vector
                def _(v, oo=sem_o.n):
                    v.wait_ge(sem_o.h, oo)

                for tt8 in range(8):
                    tt = sup * 8 + tt8

                    @blk.tensor
                    def _(t, tt=tt):
                        t.wait_ge(sem_v.h, sem_v.n)
                        for dch in range(NCH):
                            ins = t.matmul(ps_r[:], lnx[dch][:, tt * 128:(tt + 1) * 128],
                                           centT_sb[dch][:], start=(dch == 0),
                                           stop=(dch == NCH - 1))
                        ins.then_inc(sem_mm.h, 1)
                        sem_mm.n += 1

                    @blk.vector
                    def _(v):
                        v.wait_ge(sem_mm.h, sem_mm.n)
                        v.tensor_add(out=rt[:], in0=ps_r[:], in1=eb_t[:]).then_inc(sem_v.h, 1)
                        sem_v.n += 1
                        v.wait_ge(sem_v.h, sem_v.n)
                        v.tensor_reduce(out=negm[:], in_=rt[:], axis=AX.X,
                                        op=ALU.max, negate=True).then_inc(sem_v.h, 1)
                        sem_v.n += 1

                    @blk.scalar
                    def _(s):
                        s.wait_ge(sem_v.h, sem_v.n)
                        s.activation(out=ex_t[:], in_=rt[:], func=AF.Exp,
                                     bias=negm[:], scale=1.0,
                                     accum_out=denc[:]).then_inc(sem_a.h, 1)
                        sem_a.n += 1

                    @blk.vector
                    def _(v, tt8=tt8):
                        v.wait_ge(sem_a.h, sem_a.n)
                        v.reciprocal(out=rdc[:], in_=denc[:]).then_inc(sem_v.h, 1)
                        sem_v.n += 1
                        v.wait_ge(sem_v.h, sem_v.n)
                        v.tensor_scalar(out=s_t[:], in0=ex_t[:], scalar1=rdc[:],
                                        scalar2=None, op0=ALU.mult).then_inc(sem_v.h, 1)
                        sem_v.n += 1
                        v.wait_ge(sem_v.h, sem_v.n)
                        v.tensor_reduce(out=m1c[:], in_=s_t[:], axis=AX.X,
                                        op=ALU.max).then_inc(sem_v.h, 1)
                        sem_v.n += 1
                        v.wait_ge(sem_v.h, sem_v.n)
                        v.tensor_scalar(out=eq_t[:], in0=s_t[:], scalar1=m1c[:],
                                        scalar2=None, op0=ALU.is_ge).then_inc(sem_v.h, 1)
                        sem_v.n += 1
                        v.wait_ge(sem_v.h, sem_v.n)
                        v.tensor_reduce(out=cntc[:], in_=eq_t[:], axis=AX.X,
                                        op=ALU.add)
                        v.scalar_tensor_tensor(out=s2_t[:], in0=eq_t[:], scalar=-2.0,
                                               in1=s_t[:], op0=ALU.mult,
                                               op1=ALU.add).then_inc(sem_v.h, 1)
                        sem_v.n += 1
                        v.wait_ge(sem_v.h, sem_v.n)
                        v.tensor_reduce(out=m2c[:], in_=s2_t[:], axis=AX.X, op=ALU.max)
                        v.tensor_scalar(out=ge2c[:], in0=cntc[:], scalar1=2.0,
                                        scalar2=None, op0=ALU.is_ge)
                        v.tensor_sub(out=dmc[:], in0=m1c[:], in1=m2c[:]).then_inc(sem_v.h, 1)
                        sem_v.n += 1
                        v.wait_ge(sem_v.h, sem_v.n)
                        v.tensor_mul(out=dmc[:], in0=dmc[:], in1=ge2c[:]).then_inc(sem_v.h, 1)
                        sem_v.n += 1
                        v.wait_ge(sem_v.h, sem_v.n)
                        v.tensor_add(out=m2c[:], in0=m2c[:], in1=dmc[:]).then_inc(sem_v.h, 1)
                        sem_v.n += 1
                        v.wait_ge(sem_v.h, sem_v.n)
                        v.tensor_scalar(out=mask_t[:], in0=s_t[:], scalar1=m2c[:],
                                        scalar2=None, op0=ALU.is_ge).then_inc(sem_v.h, 1)
                        sem_v.n += 1
                        v.wait_ge(sem_v.h, sem_v.n)
                        v.tensor_mul(out=g_sup[:, tt8 * 16:(tt8 + 1) * 16],
                                     in0=s_t[:], in1=mask_t[:]).then_inc(sem_v.h, 1)
                        sem_v.n += 1
                        v.wait_ge(sem_v.h, sem_v.n)
                        v.tensor_copy(out=gb_t[:],
                                      in_=g_sup[:, tt8 * 16:(tt8 + 1) * 16]
                                      ).then_inc(sem_v.h, 1)
                        sem_v.n += 1

                    @blk.tensor
                    def _(t):
                        t.wait_ge(sem_v.h, sem_v.n)
                        t.transpose(ps_gtr[:], gb_t[:], identb[:]).then_inc(sem_mm.h, 1)
                        sem_mm.n += 1

                    @blk.vector
                    def _(v, tt8=tt8):
                        v.wait_ge(sem_mm.h, sem_mm.n)
                        v.tensor_copy(out=gT_sb[tt8][:], in_=ps_gtr[:]).then_inc(sem_v.h, 1)
                        sem_v.n += 1

                for hch in range(2):
                    @blk.gpsimd
                    def _(g, hch=hch, guard=sem_mm.n):
                        g.wait_ge(sem_mm.h, guard)
                        for dch in range(NCH):
                            for se in range(2):
                                g.dma_start(
                                    out=wst_sb[dch][:, se * 512:(se + 1) * 512],
                                    in_=wst[se, dch * 128:(dch + 1) * 128,
                                            hch * 512:(hch + 1) * 512]
                                    ).then_inc(sem_w.h, 16)
                                sem_w.n += 16

                    for tt8 in range(8):
                        tt = sup * 8 + tt8
                        ps_i = ps_main if (tt8 * 2 + hch) % 2 == 0 else ps_aux
                        @blk.tensor
                        def _(t, tt=tt, hch=hch, ps_i=ps_i, ww=sem_w.n):
                            t.wait_ge(sem_w.h, ww)
                            t.wait_ge(sem_v.h, sem_v.n)
                            for se in range(2):
                                for dch in range(NCH):
                                    ins = t.matmul(
                                        ps_i[:], lnx[dch][:, tt * 128:(tt + 1) * 128],
                                        wst_sb[dch][:, se * 512:(se + 1) * 512],
                                        start=(se == 0 and dch == 0),
                                        stop=(se == 1 and dch == NCH - 1))
                            ins.then_inc(sem_mm.h, 1)
                            sem_mm.n += 1

                        @blk.vector
                        def _(v, tt8=tt8, hch=hch, ps_i=ps_i):
                            v.wait_ge(sem_mm.h, sem_mm.n)
                            o0 = tt8 * 1024 + hch * 512
                            v.scalar_tensor_tensor(out=acc8[:, o0:o0 + 512],
                                                   in0=ps_i[:], scalar=0.5,
                                                   in1=bs_t[:, hch * 512:(hch + 1) * 512],
                                                   op0=ALU.mult, op1=ALU.add
                                                   ).then_inc(sem_v.h, 1)
                            sem_v.n += 1

                for tt8 in range(8):
                    for hch in range(2):
                        ps_i = ps_main if (tt8 * 2 + hch) % 2 == 0 else ps_aux
                        @blk.tensor
                        def _(t, tt8=tt8, hch=hch, ps_i=ps_i):
                            t.wait_ge(sem_v.h, sem_v.n)
                            t.matmul(ps_i[:], gT_sb[tt8][:],
                                     brb[:, hch * 512:(hch + 1) * 512],
                                     start=True, stop=True).then_inc(sem_mm.h, 1)
                            sem_mm.n += 1

                        @blk.vector
                        def _(v, tt8=tt8, hch=hch, ps_i=ps_i):
                            v.wait_ge(sem_mm.h, sem_mm.n)
                            o0 = tt8 * 1024 + hch * 512
                            v.wait_ge(sem_v.h, sem_v.n)
                            v.tensor_add(out=acc8[:, o0:o0 + 512],
                                         in0=acc8[:, o0:o0 + 512],
                                         in1=ps_i[:]).then_inc(sem_v.h, 1)
                            sem_v.n += 1

                for e in range(16):
                    @blk.sync
                    def _(sync, e=e, guard=sem_mm.n, wc=wcast_done):
                        sync.wait_ge(sem_mm.h, guard)
                        sync.wait_ge(sem_g.h, wc)
                        for dch in range(NCH):
                            sync.dma_start(out=wslE[dch][:],
                                           in_=wrtbf[e, dch * 128:(dch + 1) * 128, :]
                                           ).then_inc(sem_w.h, 16)
                            sem_w.n += 16

                    for tt8 in range(8):
                        tt = sup * 8 + tt8
                        for hch in range(2):
                            ps_i = ps_main if (tt8 * 2 + hch) % 2 == 0 else ps_aux
                            @blk.tensor
                            def _(t, tt=tt, hch=hch, ps_i=ps_i, ww=sem_w.n):
                                t.wait_ge(sem_w.h, ww)
                                t.wait_ge(sem_v.h, sem_v.n)
                                for dch in range(NCH):
                                    ins = t.matmul(ps_i[:],
                                                   lnx[dch][:, tt * 128:(tt + 1) * 128],
                                                   wslE[dch][:, hch * 512:(hch + 1) * 512],
                                                   start=(dch == 0), stop=(dch == NCH - 1))
                                ins.then_inc(sem_mm.h, 1)
                                sem_mm.n += 1

                            @blk.vector
                            def _(v, e=e, tt8=tt8, hch=hch, ps_i=ps_i):
                                v.wait_ge(sem_mm.h, sem_mm.n)
                                v.wait_ge(sem_v.h, sem_v.n)
                                o0 = tt8 * 1024 + hch * 512
                                v.scalar_tensor_tensor(
                                    out=acc8[:, o0:o0 + 512],
                                    in0=ps_i[:],
                                    scalar=g_sup[:, tt8 * 16 + e:tt8 * 16 + e + 1],
                                    in1=acc8[:, o0:o0 + 512],
                                    op0=ALU.mult, op1=ALU.add).then_inc(sem_v.h, 1)
                                sem_v.n += 1

                for tt8 in range(8):
                    tt = sup * 8 + tt8
                    for dch in range(NCH):
                        @blk.tensor
                        def _(t, tt8=tt8, dch=dch):
                            t.wait_ge(sem_v.h, sem_v.n)
                            t.transpose(ps_tr[:],
                                        acc8[:, tt8 * 1024 + dch * 128:
                                             tt8 * 1024 + (dch + 1) * 128],
                                        identf[:]).then_inc(sem_mm.h, 1)
                            sem_mm.n += 1

                        @blk.vector
                        def _(v, tt8=tt8, tt=tt, dch=dch):
                            v.wait_ge(sem_mm.h, sem_mm.n)
                            v.tensor_add(out=acc8[:, tt8 * 1024 + dch * 128:
                                              tt8 * 1024 + (dch + 1) * 128],
                                         in0=ps_tr[:],
                                         in1=lnx[dch][:, tt * 128:(tt + 1) * 128]
                                         ).then_inc(sem_v.h, 1)
                            sem_v.n += 1

                    @blk.gpsimd
                    def _(g, tt=tt, tt8=tt8, t0=t0, vv=sem_v.n,
                          first=not accum_started):
                        g.wait_ge(sem_v.h, vv)
                        if first:
                            g.wait_ge(sem_o.h, 16 * NCH)
                        g.dma_start(out=_fm_scatter_ap(xout, t0 + tt * 128),
                                    in_=acc8[:, tt8 * 1024:(tt8 + 1) * 1024],
                                    accum_op=ALU.add).then_inc(sem_o.h, 16)
                        sem_o.n += 16
                    accum_started = True

        @blk.sync
        def _(sync):
            sync.wait_ge(sem_o.h, sem_o.n)

    p.es.close()
    return p.nc


def _fm_scatter_ap(xout, tcol):
    from concourse.ap import AP as _AP
    return _AP(tensor=xout.tensor, offset=tcol,
               ap=[[T_LOC, 128], [128 * T_LOC, NCH], [1, 128]])


def build_l3_ffn():
    p = Prog("l3_ffn")
    nc = p.nc
    x_hbm = p.inp("x_fm", [NCH, 128, T_LOC])
    qT = p.inp("qT", [6, D, D], I32)
    scT = p.inp("scTrep", [6, D, D])
    laT = p.inp("laT", [6, D, 32])
    lbT = p.inp("lbT", [6, 32, D])
    biases = p.inp("L3_ffn_biases", [6, D])
    ln_g = p.inp("L3_ffn_ln_gamma", [3, D])
    ln_b = p.inp("L3_ffn_ln_beta", [3, D])
    c_onescol = p.inp("C_ones_col", [128, 1])
    c_onesrow = p.inp("C_ones_row", [1, 128])
    xout = p.out("xout_fm", [NCH, 128, T_LOC])

    ps_main = p.ps("ps_main", [128, 512])
    ps_aux = p.ps("ps_aux", [128, 512])
    ps_row = p.ps("ps_row", [1, 512])
    ps_row2 = p.ps("ps_row2", [1, 512])
    ps_r32 = p.ps("ps_r32", [32, 512])

    arena = p.sb("arena", [128, 16 * 1024], F32)
    arena_b = arena[:].bitcast(BF16)
    x_ch = [arena[:, c * 2048:(c + 1) * 2048] for c in range(NCH)]
    y1 = [arena_b[:, c * 2048:(c + 1) * 2048] for c in range(NCH)]
    y1g = [arena_b[:, 16384 + c * 2048:16384 + (c + 1) * 2048] for c in range(NCH)]
    h_ch = [p.sb(f"h{c}", [128, S], BF16) for c in range(NCH)]
    wq = [p.sb(f"wq{c}", [128, 1024], BF16) for c in range(NCH)]
    qraw = p.sb("qraw", [128, 1024], I32)
    qf32 = p.sb("qf32", [128, 1024], F32)
    sct_sb = p.sb("sct_sb", [128, 1024], BF16)
    laT_sb = [p.sb(f"laT{c}", [128, 32], BF16) for c in range(NCH)]
    lbT_sb = p.sb("lbT_sb", [32, D], BF16)
    rfm = p.sb("rfm", [32, S], BF16)
    bias_c = [p.sb(f"bia{c}", [128, 1], F32) for c in range(NCH)]
    g_c = [p.sb(f"g{c}", [128, 1], F32) for c in range(NCH)]
    bb_c = [p.sb(f"bb{c}", [128, 1], F32) for c in range(NCH)]
    ytmp = p.sb("ytmp", [128, 512], F32)
    x2b = p.sb("x2b", [128, 512], BF16)
    mrow = p.sb("mrow", [1, 512], F32)
    vrow = p.sb("vrow", [1, 512], F32)
    arow = p.sb("arow", [1, 512], F32)
    brow = p.sb("brow", [1, 512], F32)
    dtile = p.sb("dtile", [128, 512], F32)
    ones_col = p.sb("ones_col", [128, 1], F32)
    ones_col_b = p.sb("ones_col_b", [128, 1], BF16)
    ones_row = p.sb("ones_row", [1, 128], F32)
    eps_col = p.sb("eps_col", [128, 1], F32)

    sem_x = p.sem("sx")
    sem_g = p.sem("sg")
    sem_w = p.sem("sw")
    sem_mm = p.sem("smm")
    sem_v = p.sem("sv")
    sem_a = p.sem("sa")
    sem_o = p.sem("so")
    sems = {"mm": sem_mm, "v": sem_v, "a": sem_a}

    with nc.Block() as blk:
        @blk.sync
        def _(sync):
            sync.dma_start(out=ones_col[:], in_=c_onescol).then_inc(sem_x.h, 16)
            sync.dma_start(out=ones_row[:], in_=c_onesrow).then_inc(sem_x.h, 16)
            sem_x.n += 32
            for c in range(NCH):
                sync.dma_start(out=xout[c], in_=x_hbm[c]).then_inc(sem_o.h, 16)
                sem_o.n += 16

        @blk.gpsimd
        def _(g):
            g.dma_start(out=ones_col_b[:], in_=c_onescol).then_inc(sem_g.h, 16)
            sem_g.n += 16

        @blk.vector
        def _(v):
            v.memset(eps_col[:], 1e-5).then_inc(sem_v.h, 1)
            sem_v.n += 1

        for eng in ("vector", "scalar", "tensor"):
            def mk(eng=eng):
                def f(e):
                    e.wait_ge(sem_x.h, sem_x.n)
                    e.wait_ge(sem_g.h, sem_g.n)
                    e.wait_ge(sem_v.h, sem_v.n)
                return f
            getattr(blk, eng)(mk())

        cst = {"ones_col": ones_col, "ones_row": ones_row, "eps_col": eps_col}
        psb = {"row": ps_row, "row2": ps_row2, "main": ps_main, "aux": ps_aux}

        def load_matrix(l, guard_mm):
            @blk.gpsimd
            def _(g, l=l, guard=guard_mm):
                g.wait_ge(sem_mm.h, guard)
                g.dma_start(out=lbT_sb[:], in_=lbT[l]).then_inc(sem_g.h, 16)
                sem_g.n += 16
                for c in range(NCH):
                    g.dma_start(out=laT_sb[c][:], in_=laT[l, c * 128:(c + 1) * 128, :]
                                ).then_inc(sem_g.h, 16)
                    g.dma_start(out=bias_c[c][:],
                                in_=biases[l, c * 128:(c + 1) * 128].unsqueeze(1)
                                ).then_inc(sem_g.h, 16)
                    sem_g.n += 32
            for dch in range(NCH):
                @blk.sync
                def _(sync, l=l, dch=dch, guard=sem_v.n):
                    sync.wait_ge(sem_v.h, guard)
                    sync.dma_start(out=qraw[:], in_=qT[l, dch * 128:(dch + 1) * 128, :]
                                   ).then_inc(sem_x.h, 16)
                    sem_x.n += 16

                @blk.gpsimd
                def _(g, l=l, dch=dch, guard=sem_v.n):
                    g.wait_ge(sem_v.h, guard)
                    g.dma_start(out=sct_sb[:], in_=scT[l, dch * 128:(dch + 1) * 128, :]
                                ).then_inc(sem_g.h, 16)
                    sem_g.n += 16

                @blk.vector
                def _(v, dch=dch, lx=sem_x.n, lg=sem_g.n):
                    v.wait_ge(sem_x.h, lx)
                    v.wait_ge(sem_g.h, lg)
                    v.tensor_copy(out=qf32[:], in_=qraw[:]).then_inc(sem_v.h, 1)
                    sem_v.n += 1
                    v.wait_ge(sem_v.h, sem_v.n)
                    v.tensor_scalar(out=qf32[:], in0=qf32[:], scalar1=1.0 / 7.5,
                                    scalar2=-1.0, op0=ALU.mult,
                                    op1=ALU.add).then_inc(sem_v.h, 1)
                    sem_v.n += 1
                    v.wait_ge(sem_v.h, sem_v.n)
                    v.tensor_mul(out=wq[dch][:], in0=qf32[:],
                                 in1=sct_sb[:]).then_inc(sem_v.h, 1)
                    sem_v.n += 1

        def qlora_apply(src_ch, dst_kind, l, bi):
            for tc in range(4):
                @blk.tensor
                def _(t, tc=tc, src_ch=src_ch):
                    t.wait_ge(sem_v.h, sem_v.n)
                    t.wait_ge(sem_a.h, sem_a.n)
                    for dch in range(NCH):
                        ins = t.matmul(ps_r32[:], laT_sb[dch][:],
                                       src_ch[dch][:, tc * 512:(tc + 1) * 512],
                                       start=(dch == 0), stop=(dch == NCH - 1))
                    ins.then_inc(sem_mm.h, 1)
                    sem_mm.n += 1

                @blk.vector
                def _(v, tc=tc):
                    v.wait_ge(sem_mm.h, sem_mm.n)
                    v.tensor_copy(out=rfm[:, tc * 512:(tc + 1) * 512],
                                  in_=ps_r32[:]).then_inc(sem_v.h, 1)
                    sem_v.n += 1

            for och in range(NCH):
                for tc in range(4):
                    @blk.tensor
                    def _(t, och=och, tc=tc, src_ch=src_ch):
                        t.wait_ge(sem_v.h, sem_v.n)
                        t.wait_ge(sem_a.h, sem_a.n)
                        for dch in range(NCH):
                            ins = t.matmul(ps_main[:],
                                           wq[dch][:, och * 128:(och + 1) * 128],
                                           src_ch[dch][:, tc * 512:(tc + 1) * 512],
                                           start=(dch == 0), stop=(dch == NCH - 1))
                        ins.then_inc(sem_mm.h, 1)
                        sem_mm.n += 1
                        t.matmul(ps_aux[:], lbT_sb[:, och * 128:(och + 1) * 128],
                                 rfm[:, tc * 512:(tc + 1) * 512],
                                 start=True, stop=True).then_inc(sem_mm.h, 1)
                        sem_mm.n += 1

                    @blk.scalar
                    def _(s, och=och):
                        s.wait_ge(sem_mm.h, sem_mm.n)
                        s.wait_ge(sem_v.h, sem_v.n)
                        s.activation(out=ytmp[:], in_=ps_main[:], func=AF.Identity,
                                     bias=bias_c[och][:], scale=1.0).then_inc(sem_a.h, 1)
                        sem_a.n += 1

                    if dst_kind == "y1":
                        @blk.vector
                        def _(v, och=och, tc=tc):
                            v.wait_ge(sem_a.h, sem_a.n)
                            v.tensor_add(out=y1[och][:, tc * 512:(tc + 1) * 512],
                                         in0=ytmp[:], in1=ps_aux[:]).then_inc(sem_v.h, 1)
                            sem_v.n += 1
                    else:
                        @blk.vector
                        def _(v, och=och, tc=tc):
                            v.wait_ge(sem_a.h, sem_a.n)
                            v.tensor_add(out=ytmp[:], in0=ytmp[:],
                                         in1=ps_aux[:]).then_inc(sem_v.h, 1)
                            sem_v.n += 1
                            v.wait_ge(sem_v.h, sem_v.n)
                            v.tensor_add(out=h_ch[och][:, tc * 512:(tc + 1) * 512],
                                         in0=h_ch[och][:, tc * 512:(tc + 1) * 512],
                                         in1=ytmp[:]).then_inc(sem_v.h, 1)
                            sem_v.n += 1

        def emit_ln_gelu(bi):
            for col in range(4):
                c0 = col * 512
                for ch in range(NCH):
                    @blk.scalar
                    def _(s, ch=ch, c0=c0):
                        s.wait_ge(sem_mm.h, sem_mm.n)
                        s.activation(out=x2b[:], in_=y1[ch][:, c0:c0 + 512],
                                     func=AF.Square).then_inc(sem_a.h, 1)
                        sem_a.n += 1

                    @blk.tensor
                    def _(t, ch=ch, c0=c0):
                        t.wait_ge(sem_a.h, sem_a.n)
                        t.matmul(ps_row[:], ones_col_b[:], y1[ch][:, c0:c0 + 512],
                                 start=(ch == 0), stop=(ch == NCH - 1))
                        t.matmul(ps_row2[:], ones_col_b[:], x2b[:],
                                 start=(ch == 0), stop=(ch == NCH - 1)).then_inc(sem_mm.h, 1)
                        sem_mm.n += 1

                @blk.vector
                def _(v):
                    v.wait_ge(sem_mm.h, sem_mm.n)
                    v.tensor_scalar(out=mrow[:], in0=ps_row[:], scalar1=1.0 / D,
                                    scalar2=None, op0=ALU.mult)
                    v.tensor_scalar(out=vrow[:], in0=ps_row2[:], scalar1=1.0 / D,
                                    scalar2=None, op0=ALU.mult)
                    v.tensor_mul(out=arow[:], in0=mrow[:], in1=mrow[:])
                    v.tensor_sub(out=vrow[:], in0=vrow[:], in1=arow[:]).then_inc(sem_v.h, 1)
                    sem_v.n += 1

                @blk.scalar
                def _(s):
                    s.wait_ge(sem_v.h, sem_v.n)
                    s.activation(out=vrow[:], in_=vrow[:], func=AF.Sqrt,
                                 bias=eps_col[0:1, :], scale=1.0).then_inc(sem_a.h, 1)
                    sem_a.n += 1

                @blk.vector
                def _(v):
                    v.wait_ge(sem_a.h, sem_a.n)
                    v.reciprocal(out=arow[:], in_=vrow[:])
                    v.tensor_mul(out=brow[:], in0=mrow[:], in1=arow[:])
                    v.tensor_scalar(out=brow[:], in0=brow[:], scalar1=-1.0,
                                    scalar2=None, op0=ALU.mult).then_inc(sem_v.h, 1)
                    sem_v.n += 1

                @blk.tensor
                def _(t):
                    t.wait_ge(sem_v.h, sem_v.n)
                    t.matmul(ps_main[:], ones_row[:], arow[:], start=True, stop=True)
                    t.matmul(ps_aux[:], ones_row[:], brow[:], start=True,
                             stop=True).then_inc(sem_mm.h, 1)
                    sem_mm.n += 1

                for ch in range(NCH):
                    @blk.vector
                    def _(v, ch=ch, c0=c0):
                        v.wait_ge(sem_mm.h, sem_mm.n)
                        v.wait_ge(sem_a.h, sem_a.n)
                        v.tensor_mul(out=ytmp[:], in0=y1[ch][:, c0:c0 + 512],
                                     in1=ps_main[:])
                        v.tensor_add(out=ytmp[:], in0=ytmp[:], in1=ps_aux[:]
                                     ).then_inc(sem_v.h, 1)
                        sem_v.n += 1

                    @blk.scalar
                    def _(s, ch=ch, c0=c0, bi=bi):
                        s.wait_ge(sem_v.h, sem_v.n)
                        s.activation(out=y1g[ch][:, c0:c0 + 512], in_=ytmp[:],
                                     func=AF.Gelu, bias=bb_c[ch][:],
                                     scale=g_c[ch][:]).then_inc(sem_a.h, 1)
                        sem_a.n += 1

        accum_started = False
        for b in range(B_LOC):
            t0 = b * S

            @blk.sync
            def _(sync, t0=t0, guard=sem_mm.n):
                sync.wait_ge(sem_mm.h, guard)
                sync.wait_ge(sem_a.h, sem_a.n)
                for c in range(NCH):
                    sync.dma_start(out=x_ch[c], in_=x_hbm[c][:, t0:t0 + S]
                                   ).then_inc(sem_x.h, 16)
                    sem_x.n += 16

            for eng in ("vector", "scalar", "tensor"):
                def mk(eng=eng):
                    def f(e):
                        e.wait_ge(sem_x.h, sem_x.n)
                    return f
                getattr(blk, eng)(mk())

            emit_layernorm(p, blk, x_ch, h_ch, sems, cst, psb, f"l3f_b{b}",
                           t_lo=0, t_hi=S)

            for bi in range(3):
                @blk.gpsimd
                def _(g, bi=bi, guard=sem_a.n):
                    g.wait_ge(sem_a.h, guard)
                    for c in range(NCH):
                        g.dma_start(out=g_c[c][:],
                                    in_=ln_g[bi, c * 128:(c + 1) * 128].unsqueeze(1)
                                    ).then_inc(sem_g.h, 16)
                        g.dma_start(out=bb_c[c][:],
                                    in_=ln_b[bi, c * 128:(c + 1) * 128].unsqueeze(1)
                                    ).then_inc(sem_g.h, 16)
                        sem_g.n += 32

                @blk.scalar
                def _(s):
                    s.wait_ge(sem_g.h, sem_g.n)

                load_matrix(2 * bi, sem_mm.n)
                qlora_apply(h_ch, "y1", 2 * bi, bi)
                if DBG in ("l3y1", "l3y1g") and bi == 0 and b == 0:
                    if DBG == "l3y1g":
                        emit_ln_gelu(bi)
                    outs = [p.out(f"dbg{i}", [128, S], BF16) for i in range(NCH)]
                    outs2 = [p.out(f"dbh{i}", [128, S], BF16) for i in range(NCH)]
                    @blk.sync
                    def _(sync):
                        for sm in (sem_v, sem_a, sem_mm, sem_g, sem_o, sem_x):
                            sync.wait_ge(sm.h, sm.n)
                        src1 = y1 if DBG == "l3y1" else y1g
                        for i in range(NCH):
                            sync.dma_start(out=outs[i], in_=src1[i]).then_inc(sem_x.h, 16)
                            sync.dma_start(out=outs2[i], in_=h_ch[i][:]).then_inc(sem_x.h, 16)
                            sem_x.n += 32
                        sync.wait_ge(sem_x.h, sem_x.n)
                    break
                emit_ln_gelu(bi)
                load_matrix(2 * bi + 1, sem_mm.n)
                qlora_apply(y1g, "resid", 2 * bi + 1, bi)

            if DBG in ("l3y1", "l3y1g"):
                break

            for och in range(NCH):
                for tc in range(4):
                    @blk.vector
                    def _(v, och=och, tc=tc):
                        v.wait_ge(sem_v.h, sem_v.n)
                        v.wait_ge(sem_o.h, sem_o.n)
                        v.tensor_copy(out=dtile[:],
                                      in_=h_ch[och][:, tc * 512:(tc + 1) * 512]
                                      ).then_inc(sem_v.h, 1)
                        sem_v.n += 1

                    @blk.gpsimd
                    def _(g, och=och, tc=tc, t0=t0, first=not accum_started):
                        g.wait_ge(sem_v.h, sem_v.n)
                        if first:
                            g.wait_ge(sem_o.h, 16 * NCH)
                        g.dma_start(out=xout[och][:, t0 + tc * 512:t0 + (tc + 1) * 512],
                                    in_=dtile[:], accum_op=ALU.add).then_inc(sem_o.h, 16)
                        sem_o.n += 16
                    accum_started = True

        @blk.sync
        def _(sync):
            sync.wait_ge(sem_o.h, sem_o.n)

    p.es.close()
    return p.nc


def build_l3_attn():
    p = Prog("l3_attn")
    nc = p.nc
    x_hbm = p.inp("x_fm", [NCH, 128, T_LOC])
    Wq = p.inp("L3_attn_Wq", [D, D])
    bq = p.inp("L3_attn_bq", [D])
    Wk = p.inp("L3_attn_Wk", [D, D])
    bk = p.inp("L3_attn_bk", [D])
    Wv = p.inp("L3_attn_Wv", [D, D])
    bv = p.inp("L3_attn_bv", [D])
    Wo = p.inp("L3_attn_Wo", [D, D])
    bo = p.inp("L3_attn_bo", [D])
    c_onescol = p.inp("C_ones_col", [128, 1])
    c_onesrow = p.inp("C_ones_row", [1, 128])
    xout = p.out("xout_fm", [NCH, 128, T_LOC])

    wbf = {nm: p.dram(f"wbf_{nm}", [D, D], BF16) for nm in ("q", "k", "v", "o")}

    ps_main = p.ps("ps_main", [128, 512])
    ps_aux = p.ps("ps_aux", [128, 512])
    ps_sim = p.ps("ps_sim", [128, 512])
    ps_ra = [p.ps(f"ps_ra{i}", [1, 512]) for i in range(4)]
    ps_rm = p.ps("ps_rm", [128, 128])
    ps_col = ps_rm[:, 0:1]

    arena = p.sb("arena", [128, 16 * 1024], F32)
    arena_b = arena[:].bitcast(BF16)
    x_ch = [arena[:, c * 2048:(c + 1) * 2048] for c in range(NCH)]
    qf = [arena_b[:, c * 2048:(c + 1) * 2048] for c in range(NCH)]
    kf = [arena_b[:, 16384 + c * 2048:16384 + (c + 1) * 2048] for c in range(NCH)]
    lnx = [p.sb(f"lnx{c}", [128, S], BF16) for c in range(NCH)]
    vf = [p.sb(f"vf{c}", [128, S], BF16) for c in range(NCH)]
    etile = p.sb("etile", [128, 2048], BF16)
    qg_sb = p.sb("qg_sb", [64, 2048], BF16)
    kg_sb = p.sb("kg_sb", [64, 2048], BF16)
    wsl = [p.sb(f"wsl{c}", [128, 512], BF16) for c in range(NCH)]
    q2col = p.sb("q2col", [128, 512], BF16)
    r_row = p.sb("r_row", [1, 2048], BF16)
    nmx = p.sb("nmx", [1, 4], F32)
    tmp11 = p.sb("tmp11", [1, 1], F32)
    negb1 = p.sb("negb1", [1, 1], F32)
    negbq_col = p.sb("negbq_col", [128, 1], F32)
    den_col = p.sb("den_col", [128, 1], F32)
    dpart = p.sb("dpart", [128, 1], F32)
    w_colb = p.sb("w_colb", [128, 1], BF16)
    bias_c = {nm: [p.sb(f"b{nm}{c}", [128, 1], F32) for c in range(NCH)]
              for nm in ("q", "k", "v", "o")}
    qtile = p.sb("qtile", [128, 512], BF16)
    dtile = p.sb("dtile", [128, 512], F32)
    ones_col = p.sb("ones_col", [128, 1], F32)
    ones_col_b = p.sb("ones_col_b", [128, 1], BF16)
    ones_row = p.sb("ones_row", [1, 128], F32)
    ones_row_b = p.sb("ones_row_b", [1, 128], BF16)
    eps_col = p.sb("eps_col", [128, 1], F32)

    sem_x = p.sem("sx")
    sem_g = p.sem("sg")
    sem_w = p.sem("sw")
    sem_mm = p.sem("smm")
    sem_v = p.sem("sv")
    sem_a = p.sem("sa")
    sem_o = p.sem("so")
    sems = {"mm": sem_mm, "v": sem_v, "a": sem_a}

    with nc.Block() as blk:
        @blk.gpsimd
        def _(g):
            for nm, w in (("q", Wq), ("k", Wk), ("v", Wv), ("o", Wo)):
                g.dma_start(out=wbf[nm], in_=w).then_inc(sem_g.h, 16)
                sem_g.n += 16
            for nm, bb in (("q", bq), ("k", bk), ("v", bv), ("o", bo)):
                for c in range(NCH):
                    g.dma_start(out=bias_c[nm][c][:],
                                in_=bb[c * 128:(c + 1) * 128].unsqueeze(1)
                                ).then_inc(sem_g.h, 16)
                    sem_g.n += 16
            g.dma_start(out=ones_col_b[:], in_=c_onescol).then_inc(sem_g.h, 16)
            g.dma_start(out=ones_row_b[:], in_=c_onesrow).then_inc(sem_g.h, 16)
            sem_g.n += 32

        @blk.sync
        def _(sync):
            sync.dma_start(out=ones_col[:], in_=c_onescol).then_inc(sem_x.h, 16)
            sync.dma_start(out=ones_row[:], in_=c_onesrow).then_inc(sem_x.h, 16)
            sem_x.n += 32
            for c in range(NCH):
                sync.dma_start(out=xout[c], in_=x_hbm[c]).then_inc(sem_o.h, 16)
                sem_o.n += 16

        @blk.vector
        def _(v):
            v.memset(eps_col[:], 1e-5).then_inc(sem_v.h, 1)
            sem_v.n += 1

        for eng in ("vector", "scalar", "tensor"):
            def mk(eng=eng):
                def f(e):
                    e.wait_ge(sem_x.h, sem_x.n)
                    e.wait_ge(sem_g.h, sem_g.n)
                    e.wait_ge(sem_v.h, sem_v.n)
                return f
            getattr(blk, eng)(mk())

        cst = {"ones_col": ones_col, "ones_row": ones_row, "eps_col": eps_col}
        psb = {"row": ps_ra[0], "row2": ps_ra[1], "main": ps_main, "aux": ps_aux}
        wcast_done = sem_g.n

        def wslice_load(nm, o0, width, mm_guard):
            @blk.sync
            def _(sync, nm=nm, o0=o0, width=width, guard=mm_guard):
                sync.wait_ge(sem_mm.h, guard)
                sync.wait_ge(sem_g.h, wcast_done)
                for dch in range(NCH):
                    sync.dma_start(out=wsl[dch][:, 0:width],
                                   in_=wbf[nm][dch * 128:(dch + 1) * 128, o0:o0 + width]
                                   ).then_inc(sem_w.h, 16)
                    sem_w.n += 16

        accum_started = False
        for b in range(B_LOC):
            t0 = b * S

            @blk.sync
            def _(sync, t0=t0, guard=sem_mm.n):
                sync.wait_ge(sem_mm.h, guard)
                for c in range(NCH):
                    sync.dma_start(out=x_ch[c], in_=x_hbm[c][:, t0:t0 + S]
                                   ).then_inc(sem_x.h, 16)
                    sem_x.n += 16

            for eng in ("vector", "scalar", "tensor"):
                def mk(eng=eng):
                    def f(e):
                        e.wait_ge(sem_x.h, sem_x.n)
                    return f
                getattr(blk, eng)(mk())

            emit_layernorm(p, blk, x_ch, lnx, sems, cst, psb, f"l3a_b{b}",
                           t_lo=0, t_hi=S)

            if DBG == "l3ln" and b == 0:
                outs = [p.out(f"dbg{i}", [128, S], BF16) for i in range(NCH)]
                @blk.sync
                def _(sync):
                    for sm in (sem_v, sem_a, sem_mm, sem_g, sem_o, sem_x):
                        sync.wait_ge(sm.h, sm.n)
                    for i in range(NCH):
                        sync.dma_start(out=outs[i], in_=lnx[i][:]).then_inc(sem_x.h, 16)
                        sem_x.n += 16
                    sync.wait_ge(sem_x.h, sem_x.n)
                break

            _dbg1 = DBG is not None and DBG.startswith("l3q1")
            _projs = (("q", 1.0 / 32.0, qf), ("k", 1.0 / 32.0, kf), ("v", 1.0, vf))
            if _dbg1:
                _projs = (("q", 1.0 / 32.0, qf),)
            for nm, scl, dsts in _projs:
                for og in range(1 if _dbg1 else 2):
                    if DBG != "l3q1c":
                        wslice_load(nm, og * 512, 512, sem_mm.n)
                    for ol in range(1 if _dbg1 else 4):
                        och = og * 4 + ol
                        for tc in range(1 if _dbg1 else 4):
                            @blk.tensor
                            def _(t, ol=ol, tc=tc, ww=sem_w.n):
                                if DBG != "l3q1c":
                                    t.wait_ge(sem_w.h, ww)
                                t.wait_ge(sem_a.h, sem_a.n)
                                t.wait_ge(sem_v.h, sem_v.n)
                                if DBG == "l3q1c":
                                    ins = t.matmul(ps_main[:], lnx[0][:, 0:128],
                                                   lnx[0][:, 0:512],
                                                   start=True, stop=True)
                                else:
                                    for dch in range(NCH):
                                        ins = t.matmul(
                                            ps_main[:], wsl[dch][:, ol * 128:(ol + 1) * 128],
                                            lnx[dch][:, tc * 512:(tc + 1) * 512],
                                            start=(dch == 0), stop=(dch == NCH - 1))
                                ins.then_inc(sem_mm.h, 1)
                                sem_mm.n += 1

                            if DBG == "l3q1b":
                                @blk.vector
                                def _(v):
                                    v.wait_ge(sem_mm.h, sem_mm.n)
                                    v.tensor_copy(out=dtile[:], in_=ps_main[:]
                                                  ).then_inc(sem_v.h, 1)
                                    sem_v.n += 1
                            else:
                                @blk.scalar
                                def _(s, nm=nm, och=och, tc=tc, scl=scl, dsts=dsts):
                                    s.wait_ge(sem_mm.h, sem_mm.n)
                                    s.activation(out=dsts[och][:, tc * 512:(tc + 1) * 512],
                                                 in_=ps_main[:], func=AF.Identity,
                                                 bias=bias_c[nm][och][:],
                                                 scale=1.0).then_inc(sem_a.h, 1)
                                    sem_a.n += 1

                            if scl != 1.0 and DBG not in ("l3q1a", "l3q1b"):
                                @blk.vector
                                def _(v, och=och, tc=tc, scl=scl, dsts=dsts):
                                    v.wait_ge(sem_a.h, sem_a.n)
                                    v.tensor_scalar(
                                        out=dsts[och][:, tc * 512:(tc + 1) * 512],
                                        in0=dsts[och][:, tc * 512:(tc + 1) * 512],
                                        scalar1=scl, scalar2=None,
                                        op0=ALU.mult).then_inc(sem_v.h, 1)
                                    sem_v.n += 1

            if _dbg1 and b == 0:
                outs = [p.out("dbgq", [128, 512], BF16)]
                @blk.sync
                def _(sync):
                    for sm in (sem_v, sem_a, sem_mm, sem_g, sem_o, sem_x):
                        sync.wait_ge(sm.h, sm.n)
                    sync.dma_start(out=outs[0], in_=qf[0][:, 0:512]).then_inc(sem_x.h, 16)
                    sem_x.n += 16
                    sync.wait_ge(sem_x.h, sem_x.n)
                break

            @blk.vector
            def _(v):
                v.wait_ge(sem_v.h, sem_v.n)
                v.wait_ge(sem_a.h, sem_a.n)
                v.memset(nmx[:], 0.0).then_inc(sem_v.h, 1)
                sem_v.n += 1

            for src_i, src in enumerate((qf, kf)):
                for ch in range(NCH):
                    for cc in range(4):
                        @blk.scalar
                        def _(s, src=src, ch=ch, cc=cc, vv=sem_v.n):
                            s.wait_ge(sem_mm.h, sem_mm.n)
                            s.wait_ge(sem_v.h, vv)
                            s.activation(out=q2col[:],
                                         in_=src[ch][:, cc * 512:(cc + 1) * 512],
                                         func=AF.Square).then_inc(sem_a.h, 1)
                            sem_a.n += 1

                        for half in range(2):
                            @blk.tensor
                            def _(t, half=half):
                                t.wait_ge(sem_a.h, sem_a.n)
                                t.wait_ge(sem_v.h, sem_v.n)
                                t.matmul(ps_ra[2][:],
                                         ones_col_b[half * 64:half * 64 + 64, :],
                                         q2col[half * 64:half * 64 + 64, :],
                                         start=True, stop=True).then_inc(sem_mm.h, 1)
                                sem_mm.n += 1

                            @blk.vector
                            def _(v, src_i=src_i):
                                v.wait_ge(sem_mm.h, sem_mm.n)
                                v.tensor_reduce(out=tmp11[:], in_=ps_ra[2][:],
                                                axis=AX.X, op=ALU.max)
                                v.tensor_max(out=nmx[:, src_i:src_i + 1],
                                             in0=nmx[:, src_i:src_i + 1],
                                             in1=tmp11[:]).then_inc(sem_v.h, 1)
                                sem_v.n += 1

            @blk.vector
            def _(v):
                v.tensor_mul(out=nmx[:, 2:3], in0=nmx[:, 0:1],
                             in1=nmx[:, 1:2]).then_inc(sem_v.h, 1)
                sem_v.n += 1

            @blk.scalar
            def _(s):
                s.wait_ge(sem_v.h, sem_v.n)
                s.activation(out=nmx[:, 3:4], in_=nmx[:, 2:3],
                             func=AF.Sqrt).then_inc(sem_a.h, 1)
                sem_a.n += 1

            @blk.vector
            def _(v):
                v.wait_ge(sem_a.h, sem_a.n)
                v.tensor_scalar(out=negb1[:], in0=nmx[:, 3:4], scalar1=-1.0,
                                scalar2=None, op0=ALU.mult).then_inc(sem_v.h, 1)
                sem_v.n += 1

            @blk.tensor
            def _(t):
                t.wait_ge(sem_v.h, sem_v.n)
                t.matmul(ps_col, ones_row[:], negb1[:], start=True,
                         stop=True).then_inc(sem_mm.h, 1)
                sem_mm.n += 1

            @blk.vector
            def _(v):
                v.wait_ge(sem_mm.h, sem_mm.n)
                v.tensor_copy(out=negbq_col[:], in_=ps_col).then_inc(sem_v.h, 1)
                sem_v.n += 1

            if DBG == "l3p" and b == 0:
                outs = [p.out(f"dbg{i}", [128, S], BF16) for i in range(NCH)]
                @blk.sync
                def _(sync):
                    for sm in (sem_v, sem_a, sem_mm, sem_g, sem_o, sem_x):
                        sync.wait_ge(sm.h, sm.n)
                    for i in range(NCH):
                        sync.dma_start(out=outs[i], in_=qf[i]).then_inc(sem_x.h, 16)
                        sem_x.n += 16
                    sync.wait_ge(sem_x.h, sem_x.n)
                break

            ng_dbg = 1 if DBG == "l3g0" else 16
            for g in range(ng_dbg):
                @blk.vector
                def _(v, g=g):
                    v.wait_ge(sem_v.h, sem_v.n)
                    v.wait_ge(sem_mm.h, sem_mm.n)
                    for cg in range(16):
                        ch, ro = cg // 2, (cg % 2) * 64
                        v.tensor_copy(out=qg_sb[:, cg * 128:(cg + 1) * 128],
                                      in_=qf[ch][ro:ro + 64, g * 128:(g + 1) * 128])
                        ins = v.tensor_copy(out=kg_sb[:, cg * 128:(cg + 1) * 128],
                                            in_=kf[ch][ro:ro + 64, g * 128:(g + 1) * 128])
                    ins.then_inc(sem_v.h, 1)
                    sem_v.n += 1

                for cgk in range(16):
                    for a4 in range(4):
                        @blk.tensor
                        def _(t, cgk=cgk, a4=a4):
                            t.wait_ge(sem_v.h, sem_v.n)
                            t.wait_ge(sem_a.h, sem_a.n)
                            t.matmul(ps_sim[:],
                                     kg_sb[:, cgk * 128:(cgk + 1) * 128],
                                     qg_sb[:, a4 * 512:(a4 + 1) * 512],
                                     start=True, stop=True).then_inc(sem_mm.h, 1)
                            sem_mm.n += 1

                        @blk.scalar
                        def _(s, a4=a4):
                            s.wait_ge(sem_mm.h, sem_mm.n)
                            s.activation(out=etile[:, a4 * 512:(a4 + 1) * 512],
                                         in_=ps_sim[:], func=AF.Exp,
                                         bias=negbq_col[:], scale=1.0,
                                         accum_out=dpart[:]).then_inc(sem_a.h, 1)
                            sem_a.n += 1

                        @blk.vector
                        def _(v, a4=a4):
                            v.wait_ge(sem_a.h, sem_a.n)
                            if a4 == 0:
                                v.tensor_copy(out=den_col[:], in_=dpart[:]
                                              ).then_inc(sem_v.h, 1)
                            else:
                                v.tensor_add(out=den_col[:], in0=den_col[:],
                                             in1=dpart[:]).then_inc(sem_v.h, 1)
                            sem_v.n += 1

                    @blk.vector
                    def _(v):
                        v.wait_ge(sem_v.h, sem_v.n)
                        v.reciprocal(out=dpart[:], in_=den_col[:]).then_inc(sem_v.h, 1)
                        sem_v.n += 1
                        v.wait_ge(sem_v.h, sem_v.n)
                        v.tensor_copy(out=w_colb[:], in_=dpart[:]).then_inc(sem_v.h, 1)
                        sem_v.n += 1

                    for j4 in range(4):
                        @blk.tensor
                        def _(t, j4=j4, cgk=cgk):
                            t.wait_ge(sem_v.h, sem_v.n)
                            t.matmul(ps_ra[j4][:], w_colb[:],
                                     etile[:, j4 * 512:(j4 + 1) * 512],
                                     start=(cgk == 0),
                                     stop=(cgk == 15)).then_inc(sem_mm.h, 1)
                            sem_mm.n += 1

                @blk.vector
                def _(v):
                    v.wait_ge(sem_mm.h, sem_mm.n)
                    for j4 in range(4):
                        ins = v.tensor_copy(out=r_row[:, j4 * 512:(j4 + 1) * 512],
                                            in_=ps_ra[j4][:])
                    ins.then_inc(sem_v.h, 1)
                    sem_v.n += 1

                for ch in range(NCH):
                    @blk.tensor
                    def _(t, ch=ch, g=g):
                        t.wait_ge(sem_v.h, sem_v.n)
                        cg0, cg1 = 2 * ch, 2 * ch + 1
                        t.matmul(ps_rm[0:64, :], ones_row_b[:, 0:64],
                                 r_row[:, cg0 * 128:(cg0 + 1) * 128],
                                 start=True, stop=True)
                        t.matmul(ps_rm[64:128, :], ones_row_b[:, 0:64],
                                 r_row[:, cg1 * 128:(cg1 + 1) * 128],
                                 start=True, stop=True).then_inc(sem_mm.h, 1)
                        sem_mm.n += 1

                    @blk.vector
                    def _(v, ch=ch, g=g):
                        v.wait_ge(sem_mm.h, sem_mm.n)
                        v.tensor_mul(out=vf[ch][:, g * 128:(g + 1) * 128],
                                     in0=vf[ch][:, g * 128:(g + 1) * 128],
                                     in1=ps_rm[:]).then_inc(sem_v.h, 1)
                        sem_v.n += 1

            if DBG == "l3g0" and b == 0:
                outs = [p.out("dbg_r", [1, 2048], BF16), p.out("dbg_den", [128, 1], F32)]
                @blk.sync
                def _(sync):
                    for sm in (sem_v, sem_a, sem_mm, sem_g, sem_o, sem_x):
                        sync.wait_ge(sm.h, sm.n)
                    sync.dma_start(out=outs[0], in_=r_row[:]).then_inc(sem_x.h, 16)
                    sync.dma_start(out=outs[1], in_=den_col[:]).then_inc(sem_x.h, 16)
                    sem_x.n += 32
                    sync.wait_ge(sem_x.h, sem_x.n)
                break

            for og in range(2):
                wslice_load("o", og * 512, 512, sem_mm.n)
                for ol in range(4):
                    och = og * 4 + ol
                    for tc in range(4):
                        @blk.tensor
                        def _(t, ol=ol, tc=tc, ww=sem_w.n):
                            t.wait_ge(sem_w.h, ww)
                            t.wait_ge(sem_a.h, sem_a.n)
                            t.wait_ge(sem_v.h, sem_v.n)
                            for dch in range(NCH):
                                ins = t.matmul(
                                    ps_main[:], wsl[dch][:, ol * 128:(ol + 1) * 128],
                                    vf[dch][:, tc * 512:(tc + 1) * 512],
                                    start=(dch == 0), stop=(dch == NCH - 1))
                            ins.then_inc(sem_mm.h, 1)
                            sem_mm.n += 1

                        @blk.scalar
                        def _(s, och=och):
                            s.wait_ge(sem_mm.h, sem_mm.n)
                            s.wait_ge(sem_o.h, sem_o.n)
                            s.activation(out=dtile[:], in_=ps_main[:], func=AF.Identity,
                                         bias=bias_c["o"][och][:],
                                         scale=1.0).then_inc(sem_a.h, 1)
                            sem_a.n += 1

                        @blk.gpsimd
                        def _(g2, och=och, tc=tc, t0=t0, first=not accum_started):
                            g2.wait_ge(sem_a.h, sem_a.n)
                            if first:
                                g2.wait_ge(sem_o.h, 16 * NCH)
                            g2.dma_start(out=xout[och][:, t0 + tc * 512:t0 + (tc + 1) * 512],
                                         in_=dtile[:], accum_op=ALU.add).then_inc(sem_o.h, 16)
                            sem_o.n += 16
                        accum_started = True

        @blk.sync
        def _(sync):
            sync.wait_ge(sem_o.h, sem_o.n)

    p.es.close()
    return p.nc


def build_l1_ffn():
    p = Prog("l1_ffn")
    nc = p.nc
    x_hbm = p.inp("x_fm", [NCH, 128, T_LOC])
    rT = p.inp("rT", [D, 160])
    ebias = p.inp("L1_ffn_expert_bias", [160])
    wgT = p.inp("wgT", [160, D, 512])
    wdT = p.inp("wdT", [160, 512, D])
    wgsT = p.inp("wgsT", [D, 512])
    wdsT = p.inp("wdsT", [512, D])
    c_onescol = p.inp("C_ones_col", [128, 1])
    c_onesrow = p.inp("C_ones_row", [1, 128])
    c_ident = p.inp("C_ident", [128, 128])
    xout = p.out("xout_fm", [NCH, 128, T_LOC])

    wgbf = p.dram("wgbf", [160, D, 512], BF16)
    wdbf = p.dram("wdbf", [160, 512, D], BF16)
    wgsbf = p.dram("wgsbf", [D, 512], BF16)
    wdsbf = p.dram("wdsbf", [512, D], BF16)

    ps_main = p.ps("ps_main", [128, 512])
    ps_aux = p.ps("ps_aux", [128, 512])
    ps_r = p.ps("ps_r", [128, 160])
    ps_row = p.ps("ps_row", [1, 512])
    ps_row2 = p.ps("ps_row2", [1, 512])
    ps_tr = p.ps("ps_tr", [128, 128])

    arena = p.sb("arena", [128, 16 * 1024], F32)
    x_ch = [arena[:, c * 2048:(c + 1) * 2048] for c in range(NCH)]
    lnx = [p.sb(f"lnx{c}", [128, S], BF16) for c in range(NCH)]
    acc8 = p.sb("acc8", [128, 8 * 1024], F32)
    h1f = [p.sb(f"h1f{i}", [128, 1024], BF16) for i in range(4)]
    wg_sl = [[p.sb(f"wg{pa}_{c}", [128, 512], BF16) for c in range(NCH)]
             for pa in range(2)]
    wd_sl = [[p.sb(f"wd{pa}_{c}", [128, 1024], BF16) for c in range(4)]
             for pa in range(2)]
    rT_sb = [p.sb(f"rT{c}", [128, 160], BF16) for c in range(NCH)]
    eb_row = p.sb("eb_row", [1, 160], F32)
    eb_t = p.sb("eb_t", [128, 160], F32)
    comb = p.sb("comb", [128, 8 * 160], F32)
    p_t = p.sb("p_t", [128, 160], F32)
    w_t = p.sb("w_t", [128, 160], F32)
    eq_t = p.sb("eq_t", [128, 160], F32)
    mcol = p.sb("mcol", [128, 1], F32)
    scol = p.sb("scol", [128, 1], F32)
    identf = p.sb("identf", [128, 128], F32)
    ones_row = p.sb("ones_row", [1, 128], F32)
    ones_col = p.sb("ones_col", [128, 1], F32)
    eps_col = p.sb("eps_col", [128, 1], F32)

    sem_x = p.sem("sx")
    sem_g = p.sem("sg")
    sem_c = p.sem("sc")
    sem_w = p.sem("sw")
    sem_mm = p.sem("smm")
    sem_v = p.sem("sv")
    sem_a = p.sem("sa")
    sem_o = p.sem("so")
    sems = {"mm": sem_mm, "v": sem_v, "a": sem_a}

    with nc.Block() as blk:
        @blk.gpsimd
        def _(g):
            for c in range(NCH):
                g.dma_start(out=rT_sb[c][:], in_=rT[c * 128:(c + 1) * 128, :]
                            ).then_inc(sem_g.h, 16)
                sem_g.n += 16
            g.dma_start(out=eb_row[:], in_=ebias.unsqueeze(0)).then_inc(sem_g.h, 16)
            g.dma_start(out=identf[:], in_=c_ident).then_inc(sem_g.h, 16)
            sem_g.n += 32
            g.dma_start(out=wgsbf, in_=wgsT).then_inc(sem_c.h, 16)
            g.dma_start(out=wdsbf, in_=wdsT).then_inc(sem_c.h, 16)
            sem_c.n += 32
            for e in range(160):
                g.dma_start(out=wgbf[e], in_=wgT[e]).then_inc(sem_c.h, 16)
                g.dma_start(out=wdbf[e], in_=wdT[e]).then_inc(sem_c.h, 16)
                sem_c.n += 32

        @blk.sync
        def _(sync):
            sync.dma_start(out=ones_col[:], in_=c_onescol).then_inc(sem_x.h, 16)
            sync.dma_start(out=ones_row[:], in_=c_onesrow).then_inc(sem_x.h, 16)
            sem_x.n += 32
            for c in range(NCH):
                sync.dma_start(out=xout[c], in_=x_hbm[c]).then_inc(sem_o.h, 16)
                sem_o.n += 16

        @blk.vector
        def _(v):
            v.memset(eps_col[:], 1e-5).then_inc(sem_v.h, 1)
            sem_v.n += 1

        setup_g = sem_g.n
        for eng in ("vector", "scalar", "tensor"):
            def mk(eng=eng):
                def f(e):
                    e.wait_ge(sem_x.h, sem_x.n)
                    e.wait_ge(sem_g.h, setup_g)
                    e.wait_ge(sem_v.h, sem_v.n)
                return f
            getattr(blk, eng)(mk())

        cst = {"ones_col": ones_col, "ones_row": ones_row, "eps_col": eps_col}
        psb = {"row": ps_row, "row2": ps_row2, "main": ps_main, "aux": ps_aux}

        @blk.tensor
        def _(t):
            t.matmul(ps_r[:], ones_row[:], eb_row[:], start=True,
                     stop=True).then_inc(sem_mm.h, 1)
            sem_mm.n += 1

        @blk.vector
        def _(v):
            v.wait_ge(sem_mm.h, sem_mm.n)
            v.tensor_copy(out=eb_t[:], in_=ps_r[:]).then_inc(sem_v.h, 1)
            sem_v.n += 1

        accum_started = False
        for b in range(B_LOC):
            t0 = b * S

            @blk.sync
            def _(sync, t0=t0, guard=sem_mm.n):
                sync.wait_ge(sem_mm.h, guard)
                for c in range(NCH):
                    sync.dma_start(out=x_ch[c], in_=x_hbm[c][:, t0:t0 + S]
                                   ).then_inc(sem_x.h, 16)
                    sem_x.n += 16

            for eng in ("vector", "scalar", "tensor"):
                def mk(eng=eng):
                    def f(e):
                        e.wait_ge(sem_x.h, sem_x.n)
                    return f
                getattr(blk, eng)(mk())

            emit_layernorm(p, blk, x_ch, lnx, sems, cst, psb, f"l1f_b{b}",
                           t_lo=0, t_hi=S)

            for sup in range(2):
                @blk.vector
                def _(v, oo=sem_o.n):
                    v.wait_ge(sem_o.h, oo)

                for tt8 in range(8):
                    tt = sup * 8 + tt8

                    @blk.tensor
                    def _(t, tt=tt):
                        t.wait_ge(sem_v.h, sem_v.n)
                        for dch in range(NCH):
                            ins = t.matmul(ps_r[:], lnx[dch][:, tt * 128:(tt + 1) * 128],
                                           rT_sb[dch][:], start=(dch == 0),
                                           stop=(dch == NCH - 1))
                        ins.then_inc(sem_mm.h, 1)
                        sem_mm.n += 1

                    @blk.vector
                    def _(v):
                        v.wait_ge(sem_mm.h, sem_mm.n)
                        v.tensor_add(out=p_t[:], in0=ps_r[:], in1=eb_t[:]).then_inc(sem_v.h, 1)
                        sem_v.n += 1

                    @blk.scalar
                    def _(s):
                        s.wait_ge(sem_v.h, sem_v.n)
                        s.activation(out=p_t[:], in_=p_t[:],
                                     func=AF.Sigmoid).then_inc(sem_a.h, 1)
                        sem_a.n += 1

                    @blk.vector
                    def _(v, tt8=tt8):
                        v.wait_ge(sem_a.h, sem_a.n)
                        v.tensor_copy(out=w_t[:], in_=p_t[:]).then_inc(sem_v.h, 1)
                        sem_v.n += 1
                        for k in range(8):
                            v.wait_ge(sem_v.h, sem_v.n)
                            v.tensor_reduce(out=mcol[:], in_=w_t[:], axis=AX.X,
                                            op=ALU.max).then_inc(sem_v.h, 1)
                            sem_v.n += 1
                            if k < 7:
                                v.wait_ge(sem_v.h, sem_v.n)
                                v.tensor_scalar(out=eq_t[:], in0=w_t[:], scalar1=mcol[:],
                                                scalar2=None, op0=ALU.is_ge).then_inc(sem_v.h, 1)
                                sem_v.n += 1
                                v.wait_ge(sem_v.h, sem_v.n)
                                v.scalar_tensor_tensor(out=w_t[:], in0=eq_t[:],
                                                       scalar=-2.0, in1=w_t[:],
                                                       op0=ALU.mult,
                                                       op1=ALU.add).then_inc(sem_v.h, 1)
                                sem_v.n += 1
                        v.wait_ge(sem_v.h, sem_v.n)
                        v.tensor_scalar(out=eq_t[:], in0=p_t[:], scalar1=mcol[:],
                                        scalar2=None, op0=ALU.is_ge).then_inc(sem_v.h, 1)
                        sem_v.n += 1
                        v.wait_ge(sem_v.h, sem_v.n)
                        v.tensor_mul(out=w_t[:], in0=p_t[:], in1=eq_t[:]).then_inc(sem_v.h, 1)
                        sem_v.n += 1
                        v.wait_ge(sem_v.h, sem_v.n)
                        v.tensor_reduce(out=scol[:], in_=w_t[:], axis=AX.X,
                                        op=ALU.add).then_inc(sem_v.h, 1)
                        sem_v.n += 1
                        v.wait_ge(sem_v.h, sem_v.n)
                        v.tensor_scalar(out=scol[:], in0=scol[:], scalar1=1e-9,
                                        scalar2=None, op0=ALU.add).then_inc(sem_v.h, 1)
                        sem_v.n += 1
                        v.wait_ge(sem_v.h, sem_v.n)
                        v.reciprocal(out=mcol[:], in_=scol[:]).then_inc(sem_v.h, 1)
                        sem_v.n += 1
                        v.wait_ge(sem_v.h, sem_v.n)
                        v.tensor_scalar(out=mcol[:], in0=mcol[:], scalar1=2.5,
                                        scalar2=None, op0=ALU.mult).then_inc(sem_v.h, 1)
                        sem_v.n += 1
                        v.wait_ge(sem_v.h, sem_v.n)
                        v.tensor_scalar(out=comb[:, tt8 * 160:(tt8 + 1) * 160],
                                        in0=w_t[:], scalar1=mcol[:], scalar2=None,
                                        op0=ALU.mult).then_inc(sem_v.h, 1)
                        sem_v.n += 1

                for ei in range(161):
                    e = ei - 1
                    shared = (ei == 0)
                    pa = ei % 2
                    gsrc = wgsbf if shared else wgbf[e]
                    dsrc = wdsbf if shared else wdbf[e]

                    @blk.sync
                    def _(sync, gsrc=gsrc, dsrc=dsrc, e=e, pa=pa, guard_mm=sem_mm.n):
                        sync.wait_ge(sem_c.h, 32 * (e + 2))
                        sync.wait_ge(sem_mm.h, guard_mm)
                        for dch in range(NCH):
                            sync.dma_start(out=wg_sl[pa][dch][:],
                                           in_=gsrc[dch * 128:(dch + 1) * 128, :]
                                           ).then_inc(sem_w.h, 16)
                            sem_w.n += 16
                        for ich in range(4):
                            sync.dma_start(out=wd_sl[pa][ich][:],
                                           in_=dsrc[ich * 128:(ich + 1) * 128, :]
                                           ).then_inc(sem_w.h, 16)
                            sem_w.n += 16

                    wgx = wg_sl[pa]
                    wdx = wd_sl[pa]

                    for ich in range(4):
                        for tch in range(2):
                            ps_i = ps_main if (ich * 2 + tch) % 2 == 0 else ps_aux
                            @blk.tensor
                            def _(t, ich=ich, tch=tch, sup=sup, ps_i=ps_i,
                                  wgx=wgx, ww=sem_w.n):
                                t.wait_ge(sem_w.h, ww)
                                t.wait_ge(sem_a.h, sem_a.n)
                                t.wait_ge(sem_v.h, sem_v.n)
                                tc0 = sup * 1024 + tch * 512
                                for dch in range(NCH):
                                    ins = t.matmul(ps_i[:],
                                                   wgx[dch][:, ich * 128:(ich + 1) * 128],
                                                   lnx[dch][:, tc0:tc0 + 512],
                                                   start=(dch == 0), stop=(dch == NCH - 1))
                                ins.then_inc(sem_mm.h, 1)
                                sem_mm.n += 1

                            @blk.scalar
                            def _(s, ich=ich, tch=tch, ps_i=ps_i):
                                s.wait_ge(sem_mm.h, sem_mm.n)
                                s.activation(out=h1f[ich][:, tch * 512:(tch + 1) * 512],
                                             in_=ps_i[:],
                                             func=AF.Silu).then_inc(sem_a.h, 1)
                                sem_a.n += 1

                    for tt8 in range(8):
                        for hch in range(2):
                            ps_i = ps_main if (tt8 * 2 + hch) % 2 == 0 else ps_aux
                            @blk.tensor
                            def _(t, tt8=tt8, hch=hch, ps_i=ps_i, wdx=wdx):
                                t.wait_ge(sem_a.h, sem_a.n)
                                t.wait_ge(sem_v.h, sem_v.n)
                                for ich in range(4):
                                    ins = t.matmul(
                                        ps_i[:],
                                        h1f[ich][:, tt8 * 128:(tt8 + 1) * 128],
                                        wdx[ich][:, hch * 512:(hch + 1) * 512],
                                        start=(ich == 0), stop=(ich == 3))
                                ins.then_inc(sem_mm.h, 1)
                                sem_mm.n += 1

                            if shared:
                                @blk.vector
                                def _(v, tt8=tt8, hch=hch, ps_i=ps_i):
                                    v.wait_ge(sem_mm.h, sem_mm.n)
                                    o0 = tt8 * 1024 + hch * 512
                                    v.tensor_copy(out=acc8[:, o0:o0 + 512],
                                                  in_=ps_i[:]).then_inc(sem_v.h, 1)
                                    sem_v.n += 1
                            else:
                                @blk.vector
                                def _(v, e=e, tt8=tt8, hch=hch, ps_i=ps_i):
                                    v.wait_ge(sem_mm.h, sem_mm.n)
                                    o0 = tt8 * 1024 + hch * 512
                                    v.scalar_tensor_tensor(
                                        out=acc8[:, o0:o0 + 512], in0=ps_i[:],
                                        scalar=comb[:, tt8 * 160 + e:tt8 * 160 + e + 1],
                                        in1=acc8[:, o0:o0 + 512],
                                        op0=ALU.mult, op1=ALU.add).then_inc(sem_v.h, 1)
                                    sem_v.n += 1

                for tt8 in range(8):
                    tt = sup * 8 + tt8
                    for dch in range(NCH):
                        @blk.tensor
                        def _(t, tt8=tt8, dch=dch):
                            t.wait_ge(sem_v.h, sem_v.n)
                            t.transpose(ps_tr[:],
                                        acc8[:, tt8 * 1024 + dch * 128:
                                             tt8 * 1024 + (dch + 1) * 128],
                                        identf[:]).then_inc(sem_mm.h, 1)
                            sem_mm.n += 1

                        @blk.vector
                        def _(v, tt8=tt8, dch=dch):
                            v.wait_ge(sem_mm.h, sem_mm.n)
                            v.tensor_copy(out=acc8[:, tt8 * 1024 + dch * 128:
                                               tt8 * 1024 + (dch + 1) * 128],
                                          in_=ps_tr[:]).then_inc(sem_v.h, 1)
                            sem_v.n += 1

                    @blk.gpsimd
                    def _(g, tt=tt, tt8=tt8, t0=t0, vv=sem_v.n,
                          first=not accum_started):
                        g.wait_ge(sem_v.h, vv)
                        if first:
                            g.wait_ge(sem_o.h, 16 * NCH)
                        g.dma_start(out=_fm_scatter_ap(xout, t0 + tt * 128),
                                    in_=acc8[:, tt8 * 1024:(tt8 + 1) * 1024],
                                    accum_op=ALU.add).then_inc(sem_o.h, 16)
                        sem_o.n += 16
                    accum_started = True

        @blk.sync
        def _(sync):
            sync.wait_ge(sem_o.h, sem_o.n)

    p.es.close()
    return p.nc


SCAN_CH = 64


def build_l2_attn():
    p = Prog("l2_attn")
    nc = p.nc
    x_hbm = p.inp("x_fm", [NCH, 128, T_LOC])
    winT = p.inp("WinT", [D, D])
    projT = p.inp("projT", [D, 420])
    woutT = p.inp("WoutT", [32, D])
    bbeta = p.inp("L2_attn_b_beta", [4, 32])
    c_onescol = p.inp("C_ones_col", [128, 1])
    c_onesrow = p.inp("C_ones_row", [1, 128])
    c_ident = p.inp("C_ident", [128, 128])
    c_ind5 = p.inp("C_ind5", [5, 128])
    c_ind4 = p.inp("C_ind4", [4, 128])
    c_p32 = p.inp("C_p32", [128, 32])
    xout = p.out("xout_fm", [NCH, 128, T_LOC])

    buf55 = p.dram("buf55", [B_LOC, S, 5, 64], BF16)
    bufuw = p.dram("bufuw", [B_LOC, S, 4, 2], BF16)
    bufvT = p.dram("bufvT", [B_LOC, 128, S], F32)
    bufbT = p.dram("bufbT", [B_LOC, 128, S], F32)
    bufo = p.dram("bufo", [B_LOC, 128, S], F32)

    ps_main = p.ps("ps_main", [128, 512])
    ps_aux = p.ps("ps_aux", [128, 512])
    ps_row = p.ps("ps_row", [1, 512])
    ps_row2 = p.ps("ps_row2", [1, 512])
    ps_sc = [[p.ps(f"ps_sc{b}_{i}", [128, 66]) for i in range(2)] for b in range(B_LOC)]

    arena = p.sb("arena", [128, 16 * 1024], F32)
    arena_b = arena[:].bitcast(BF16)
    x_ch = [arena[:, c * 2048:(c + 1) * 2048] for c in range(NCH)]
    xp_fm = [arena_b[:, c * 2048:(c + 1) * 2048] for c in range(NCH)]
    lnx = [p.sb(f"lnx{c}", [128, S], BF16) for c in range(NCH)]
    wsl = [p.sb(f"wsl{c}", [128, 512], BF16) for c in range(NCH)]
    projc = [p.sb(f"projc{c}", [128, 420], BF16) for c in range(NCH)]
    bb_t = p.sb("bb_t", [128, 128], F32)
    bb_row = p.sb("bb_row", [1, 128], F32)
    kflat = p.sb("kflat", [128, 128], F32)
    knt = p.sb("knt", [128, 128], BF16)
    betat = p.sb("betat", [128, 128], F32)
    lg = p.sb("lg", [128, 4], F32)
    sm_t = p.sb("sm_t", [128, 4], F32)
    uw_t = p.sb("uw_t", [128, 4], F32)
    eq_t = p.sb("eq_t", [128, 4], F32)
    uwsm_t = p.sb("uwsm_t", [128, 8], BF16)
    ss4 = p.sb("ss4", [128, 4], F32)
    m1c = p.sb("m1c", [128, 1], F32)
    m2c = p.sb("m2c", [128, 1], F32)
    negm = p.sb("negm", [128, 1], F32)
    denc = p.sb("denc", [128, 1], F32)
    rdc = p.sb("rdc", [128, 1], F32)
    cntc = p.sb("cntc", [128, 1], F32)
    ge2c = p.sb("ge2c", [128, 1], F32)
    dmc = p.sb("dmc", [128, 1], F32)
    qsl = p.sb("qsl", [128, 32], BF16)
    ind5b = p.sb("ind5b", [5, 128], BF16)
    ind4b = p.sb("ind4b", [4, 128], BF16)
    p32b = p.sb("p32b", [128, 32], BF16)
    identb = p.sb("identb", [128, 128], BF16)
    identf = p.sb("identf", [128, 128], F32)
    wout_sb = p.sb("wout_sb", [32, D], BF16)
    zt512 = p.sb("zt512", [128, 512], BF16)
    S_st = [p.sb(f"S{b}", [128, 32], F32) for b in range(B_LOC)]
    r55c = [p.sb(f"r55c{b}", [5, SCAN_CH * 64], BF16) for b in range(B_LOC)]
    uwc = [p.sb(f"uwc{b}", [4, SCAN_CH * 2], BF16) for b in range(B_LOC)]
    vTc = [p.sb(f"vTc{b}", [128, SCAN_CH], F32) for b in range(B_LOC)]
    bTc = [p.sb(f"bTc{b}", [128, SCAN_CH], F32) for b in range(B_LOC)]
    obuf = [p.sb(f"obuf{b}", [128, SCAN_CH], F32) for b in range(B_LOC)]
    uwsb = [p.sb(f"uwsb{b}", [128, 2], F32) for b in range(B_LOC)]
    t1s = [p.sb(f"t1s{b}", [128, 32], F32) for b in range(B_LOC)]
    retr = [p.sb(f"retr{b}", [128, 1], F32) for b in range(B_LOC)]
    delta = [p.sb(f"delta{b}", [128, 1], F32) for b in range(B_LOC)]
    u2s = [p.sb(f"u2s{b}", [128, 32], F32) for b in range(B_LOC)]
    spre = [p.sb(f"spre{b}", [128, 32], F32) for b in range(B_LOC)]
    s2s = [p.sb(f"s2s{b}", [128, 32], F32) for b in range(B_LOC)]
    sqs = [p.sb(f"sq{b}", [128, 1], F32) for b in range(B_LOC)]
    sgs = [p.sb(f"sg{b}", [128, 1], F32) for b in range(B_LOC)]
    sq2s = [p.sb(f"sq2{b}", [128, 1], F32) for b in range(B_LOC)]
    celltile = p.sb("celltile", [32, 512], BF16)
    botile = p.sb("botile", [128, 512], BF16)
    dtile = p.sb("dtile", [128, 512], F32)
    ztile = p.sb("ztile", [128, 40], BF16)
    ones_row = p.sb("ones_row", [1, 128], F32)
    ones_col = p.sb("ones_col", [128, 1], F32)
    eps_col = p.sb("eps_col", [128, 1], F32)

    sem_x = p.sem("sx")
    sem_g = p.sem("sg")
    sem_w = p.sem("sw")
    sem_mm = p.sem("smm")
    sem_v = p.sem("sv")
    sem_a = p.sem("sa")
    sem_o = p.sem("so")
    sems = {"mm": sem_mm, "v": sem_v, "a": sem_a}

    with nc.Block() as blk:
        @blk.gpsimd
        def _(g):
            for c in range(NCH):
                g.dma_start(out=projc[c][:], in_=projT[c * 128:(c + 1) * 128, :]
                            ).then_inc(sem_g.h, 16)
                sem_g.n += 16
            g.dma_start(out=bb_row[:], in_=bbeta.rearrange("a b -> (a b)").unsqueeze(0)
                        ).then_inc(sem_g.h, 16)
            g.dma_start(out=ind5b[:], in_=c_ind5).then_inc(sem_g.h, 16)
            g.dma_start(out=ind4b[:], in_=c_ind4).then_inc(sem_g.h, 16)
            g.dma_start(out=p32b[:], in_=c_p32).then_inc(sem_g.h, 16)
            g.dma_start(out=identb[:], in_=c_ident).then_inc(sem_g.h, 16)
            g.dma_start(out=identf[:], in_=c_ident).then_inc(sem_g.h, 16)
            g.dma_start(out=wout_sb[:], in_=woutT).then_inc(sem_g.h, 16)
            sem_g.n += 7 * 16

        @blk.sync
        def _(sync):
            sync.dma_start(out=ones_col[:], in_=c_onescol).then_inc(sem_x.h, 16)
            sync.dma_start(out=ones_row[:], in_=c_onesrow).then_inc(sem_x.h, 16)
            sem_x.n += 32
            for c in range(NCH):
                sync.dma_start(out=xout[c], in_=x_hbm[c]).then_inc(sem_o.h, 16)
                sem_o.n += 16

        @blk.vector
        def _(v):
            v.memset(eps_col[:], 1e-5)
            v.memset(zt512[:], 0.0)
            for b in range(B_LOC):
                v.memset(S_st[b][:], 0.0)
            v.engine_nop().then_inc(sem_v.h, 1)
            sem_v.n += 1

        for eng in ("vector", "scalar", "tensor"):
            def mk(eng=eng):
                def f(e):
                    e.wait_ge(sem_x.h, sem_x.n)
                    e.wait_ge(sem_g.h, sem_g.n)
                    e.wait_ge(sem_v.h, sem_v.n)
                return f
            getattr(blk, eng)(mk())

        @blk.gpsimd
        def _(g):
            total = S * 5 * 64
            nchunk = total // (128 * 512)
            for b in range(B_LOC):
                for k in range(nchunk):
                    g.dma_start(out=_zchunk_ap(buf55, b * total + k * 128 * 512),
                                in_=zt512[:]).then_inc(sem_g.h, 16)
                    sem_g.n += 16

        @blk.tensor
        def _(t):
            t.matmul(ps_main[:, 0:128], ones_row[:], bb_row[:],
                     start=True, stop=True).then_inc(sem_mm.h, 1)
            sem_mm.n += 1

        @blk.vector
        def _(v):
            v.wait_ge(sem_mm.h, sem_mm.n)
            v.tensor_copy(out=bb_t[:], in_=ps_main[:, 0:128]).then_inc(sem_v.h, 1)
            sem_v.n += 1

        cst = {"ones_col": ones_col, "ones_row": ones_row, "eps_col": eps_col}
        psb = {"row": ps_row, "row2": ps_row2, "main": ps_main, "aux": ps_aux}

        def wslice_load(o0, width, mm_guard):
            @blk.gpsimd
            def _(g, o0=o0, width=width, guard=mm_guard):
                g.wait_ge(sem_mm.h, guard)
                for dch in range(NCH):
                    g.dma_start(out=wsl[dch][:, 0:width],
                                in_=winT[dch * 128:(dch + 1) * 128, o0:o0 + width]
                                ).then_inc(sem_w.h, 16)
                    sem_w.n += 16

        for b in range(B_LOC):
            t0 = b * S

            @blk.sync
            def _(sync, t0=t0, guard=sem_mm.n):
                sync.wait_ge(sem_mm.h, guard)
                for c in range(NCH):
                    sync.dma_start(out=x_ch[c], in_=x_hbm[c][:, t0:t0 + S]
                                   ).then_inc(sem_x.h, 16)
                    sem_x.n += 16

            for eng in ("vector", "scalar", "tensor"):
                def mk(eng=eng):
                    def f(e):
                        e.wait_ge(sem_x.h, sem_x.n)
                    return f
                getattr(blk, eng)(mk())

            emit_layernorm(p, blk, x_ch, lnx, sems, cst, psb, f"l2a_b{b}",
                           t_lo=0, t_hi=S)

            for og in range(2):
                wslice_load(og * 512, 512, sem_mm.n)
                for ol in range(4):
                    och = og * 4 + ol
                    for tc in range(4):
                        @blk.tensor
                        def _(t, ol=ol, tc=tc, ww=sem_w.n):
                            t.wait_ge(sem_w.h, ww)
                            t.wait_ge(sem_a.h, sem_a.n)
                            t.wait_ge(sem_v.h, sem_v.n)
                            for dch in range(NCH):
                                ins = t.matmul(
                                    ps_main[:], wsl[dch][:, ol * 128:(ol + 1) * 128],
                                    lnx[dch][:, tc * 512:(tc + 1) * 512],
                                    start=(dch == 0), stop=(dch == NCH - 1))
                            ins.then_inc(sem_mm.h, 1)
                            sem_mm.n += 1

                        @blk.scalar
                        def _(s, och=och, tc=tc):
                            s.wait_ge(sem_mm.h, sem_mm.n)
                            s.activation(out=xp_fm[och][:, tc * 512:(tc + 1) * 512],
                                         in_=ps_main[:],
                                         func=AF.Silu).then_inc(sem_a.h, 1)
                            sem_a.n += 1

            for tt in range(16):
                @blk.tensor
                def _(t, tt=tt):
                    t.wait_ge(sem_a.h, sem_a.n)
                    t.wait_ge(sem_v.h, sem_v.n)
                    for dch in range(NCH):
                        ins = t.matmul(ps_aux[:, 0:420],
                                       xp_fm[dch][:, tt * 128:(tt + 1) * 128],
                                       projc[dch][:], start=(dch == 0),
                                       stop=(dch == NCH - 1))
                    ins.then_inc(sem_mm.h, 1)
                    sem_mm.n += 1

                @blk.vector
                def _(v):
                    v.wait_ge(sem_mm.h, sem_mm.n)
                    v.tensor_copy(out=lg[:], in_=ps_aux[:, 0:4]).then_inc(sem_v.h, 1)
                    sem_v.n += 1
                    v.wait_ge(sem_v.h, sem_v.n)
                    v.tensor_reduce(out=negm[:], in_=lg[:], axis=AX.X,
                                    op=ALU.max, negate=True).then_inc(sem_v.h, 1)
                    sem_v.n += 1

                @blk.scalar
                def _(s):
                    s.wait_ge(sem_v.h, sem_v.n)
                    s.activation(out=sm_t[:], in_=lg[:], func=AF.Exp, bias=negm[:],
                                 scale=1.0, accum_out=denc[:]).then_inc(sem_a.h, 1)
                    sem_a.n += 1

                @blk.vector
                def _(v):
                    v.wait_ge(sem_a.h, sem_a.n)
                    v.reciprocal(out=rdc[:], in_=denc[:]).then_inc(sem_v.h, 1)
                    sem_v.n += 1
                    v.wait_ge(sem_v.h, sem_v.n)
                    v.tensor_scalar(out=sm_t[:], in0=sm_t[:], scalar1=rdc[:],
                                    scalar2=None, op0=ALU.mult).then_inc(sem_v.h, 1)
                    sem_v.n += 1
                    v.wait_ge(sem_v.h, sem_v.n)
                    v.tensor_reduce(out=m1c[:], in_=lg[:], axis=AX.X,
                                    op=ALU.max).then_inc(sem_v.h, 1)
                    sem_v.n += 1
                    v.wait_ge(sem_v.h, sem_v.n)
                    v.tensor_scalar(out=eq_t[:], in0=lg[:], scalar1=m1c[:],
                                    scalar2=None, op0=ALU.is_ge).then_inc(sem_v.h, 1)
                    sem_v.n += 1
                    v.wait_ge(sem_v.h, sem_v.n)
                    v.tensor_reduce(out=cntc[:], in_=eq_t[:], axis=AX.X, op=ALU.add)
                    v.scalar_tensor_tensor(out=ss4[:], in0=eq_t[:], scalar=-1e9,
                                           in1=lg[:], op0=ALU.mult,
                                           op1=ALU.add).then_inc(sem_v.h, 1)
                    sem_v.n += 1
                    v.wait_ge(sem_v.h, sem_v.n)
                    v.tensor_reduce(out=m2c[:], in_=ss4[:], axis=AX.X, op=ALU.max)
                    v.tensor_scalar(out=ge2c[:], in0=cntc[:], scalar1=2.0,
                                    scalar2=None, op0=ALU.is_ge)
                    v.tensor_sub(out=dmc[:], in0=m1c[:], in1=m2c[:]).then_inc(sem_v.h, 1)
                    sem_v.n += 1
                    v.wait_ge(sem_v.h, sem_v.n)
                    v.tensor_mul(out=dmc[:], in0=dmc[:], in1=ge2c[:]).then_inc(sem_v.h, 1)
                    sem_v.n += 1
                    v.wait_ge(sem_v.h, sem_v.n)
                    v.tensor_add(out=m2c[:], in0=m2c[:], in1=dmc[:]).then_inc(sem_v.h, 1)
                    sem_v.n += 1
                    v.wait_ge(sem_v.h, sem_v.n)
                    v.tensor_scalar(out=eq_t[:], in0=lg[:], scalar1=m2c[:],
                                    scalar2=None, op0=ALU.is_ge).then_inc(sem_v.h, 1)
                    sem_v.n += 1
                    v.wait_ge(sem_v.h, sem_v.n)
                    v.tensor_mul(out=uw_t[:], in0=sm_t[:], in1=eq_t[:]).then_inc(sem_v.h, 1)
                    sem_v.n += 1
                    v.wait_ge(sem_v.h, sem_v.n)
                    v.tensor_reduce(out=denc[:], in_=uw_t[:], axis=AX.X,
                                    op=ALU.add).then_inc(sem_v.h, 1)
                    sem_v.n += 1
                    v.wait_ge(sem_v.h, sem_v.n)
                    v.tensor_scalar(out=denc[:], in0=denc[:], scalar1=1e-8,
                                    scalar2=None, op0=ALU.add).then_inc(sem_v.h, 1)
                    sem_v.n += 1
                    v.wait_ge(sem_v.h, sem_v.n)
                    v.reciprocal(out=rdc[:], in_=denc[:]).then_inc(sem_v.h, 1)
                    sem_v.n += 1
                    v.wait_ge(sem_v.h, sem_v.n)
                    v.tensor_scalar(out=uw_t[:], in0=uw_t[:], scalar1=rdc[:],
                                    scalar2=None, op0=ALU.mult).then_inc(sem_v.h, 1)
                    sem_v.n += 1
                    v.wait_ge(sem_v.h, sem_v.n)
                    u3 = uwsm_t[:].rearrange("p (k two) -> p k two", two=2)
                    v.tensor_copy(out=u3[:, :, 0:1], in_=uw_t[:].unsqueeze(2))
                    v.tensor_copy(out=u3[:, :, 1:2],
                                  in_=sm_t[:].unsqueeze(2)).then_inc(sem_v.h, 1)
                    sem_v.n += 1
                    v.wait_ge(sem_v.h, sem_v.n)
                    v.tensor_copy(out=kflat[:].rearrange("p (a b) -> p a b", b=32),
                                  in_=ps_aux[:, 4:260].rearrange(
                                      "p (a c) -> p a c", c=64)[:, :, 0:32]
                                  ).then_inc(sem_v.h, 1)
                    sem_v.n += 1
                    v.wait_ge(sem_v.h, sem_v.n)
                    v.tensor_mul(out=betat[:], in0=kflat[:], in1=kflat[:])
                    v.tensor_reduce(out=ss4[:],
                                    in_=betat[:].rearrange("p (a b) -> p a b", b=32),
                                    axis=AX.X, op=ALU.add).then_inc(sem_v.h, 1)
                    sem_v.n += 1

                @blk.scalar
                def _(s):
                    s.wait_ge(sem_v.h, sem_v.n)
                    s.activation(out=ss4[:], in_=ss4[:],
                                 func=AF.Sqrt).then_inc(sem_a.h, 1)
                    sem_a.n += 1

                @blk.vector
                def _(v):
                    v.wait_ge(sem_a.h, sem_a.n)
                    v.tensor_scalar(out=ss4[:], in0=ss4[:], scalar1=1e-6,
                                    scalar2=None, op0=ALU.add).then_inc(sem_v.h, 1)
                    sem_v.n += 1
                    v.wait_ge(sem_v.h, sem_v.n)
                    v.reciprocal(out=ss4[:], in_=ss4[:]).then_inc(sem_v.h, 1)
                    sem_v.n += 1
                    v.wait_ge(sem_v.h, sem_v.n)
                    v.tensor_tensor(out=knt[:].rearrange("p (a b) -> p a b", b=32),
                                    in0=kflat[:].rearrange("p (a b) -> p a b", b=32),
                                    in1=ss4[:].unsqueeze(2).to_broadcast([128, 4, 32]),
                                    op=ALU.mult).then_inc(sem_v.h, 1)
                    sem_v.n += 1
                    v.wait_ge(sem_v.h, sem_v.n)
                    v.tensor_copy(out=kflat[:].rearrange("p (a b) -> p a b", b=32),
                                  in_=ps_aux[:, 4:260].rearrange(
                                      "p (a c) -> p a c", c=64)[:, :, 32:64]
                                  ).then_inc(sem_v.h, 1)
                    sem_v.n += 1
                    v.wait_ge(sem_v.h, sem_v.n)
                    v.tensor_copy(out=qsl[:], in_=ps_aux[:, 388:420]).then_inc(sem_v.h, 1)
                    sem_v.n += 1

                @blk.vector
                def _(v):
                    v.wait_ge(sem_v.h, sem_v.n)
                    v.tensor_add(out=betat[:], in0=ps_aux[:, 260:388],
                                 in1=bb_t[:]).then_inc(sem_v.h, 1)
                    sem_v.n += 1

                @blk.scalar
                def _(s):
                    s.wait_ge(sem_v.h, sem_v.n)
                    s.activation(out=betat[:], in_=betat[:],
                                 func=AF.Sigmoid).then_inc(sem_a.h, 1)
                    sem_a.n += 1

                for which, srcf in (("v", kflat), ("beta", betat)):
                    @blk.tensor
                    def _(t, srcf=srcf):
                        t.wait_ge(sem_a.h, sem_a.n)
                        t.wait_ge(sem_v.h, sem_v.n)
                        t.transpose(ps_main[:, 0:128], srcf[:],
                                    identf[:]).then_inc(sem_mm.h, 1)
                        sem_mm.n += 1

                    @blk.vector
                    def _(v, which=which):
                        v.wait_ge(sem_mm.h, sem_mm.n)
                        v.tensor_copy(out=betat[:] if which == "beta" else kflat[:],
                                      in_=ps_main[:, 0:128]).then_inc(sem_v.h, 1)
                        sem_v.n += 1

                    @blk.gpsimd
                    def _(g, which=which, b=b, tt=tt, vv=sem_v.n):
                        g.wait_ge(sem_v.h, vv)
                        buf = bufvT if which == "v" else bufbT
                        src = kflat if which == "v" else betat
                        g.dma_start(out=buf[b][:, tt * 128:(tt + 1) * 128],
                                    in_=src[:]).then_inc(sem_g.h, 16)
                        sem_g.n += 16

                @blk.gpsimd
                def _(g, b=b, tt=tt, vv=sem_v.n, zz=True):
                    g.wait_ge(sem_v.h, vv)
                    g.dma_start(out=_buf55_kn_ap(buf55, b, tt), in_=knt[:]
                                ).then_inc(sem_g.h, 16)
                    g.dma_start(out=_buf55_q_ap(buf55, b, tt), in_=qsl[:]
                                ).then_inc(sem_g.h, 16)
                    g.dma_start(out=bufuw[b, tt * 128:(tt + 1) * 128].rearrange(
                        "s a b -> s (a b)"), in_=uwsm_t[:]).then_inc(sem_g.h, 16)
                    sem_g.n += 48

        sv_done = [0, 0]
        for ci in range(S // SCAN_CH):
            c0 = ci * SCAN_CH
            for b in range(B_LOC):
                @blk.sync
                def _(sync, b=b, c0=c0, guard=sem_v.n, gm=sem_mm.n, gg=sem_g.n):
                    sync.wait_ge(sem_v.h, guard)
                    sync.wait_ge(sem_mm.h, gm)
                    sync.wait_ge(sem_g.h, gg)
                    sync.dma_start(out=r55c[b][:],
                                   in_=_chunk55_ap(buf55, b, c0)).then_inc(sem_x.h, 16)
                    sync.dma_start(out=uwc[b][:],
                                   in_=_chunkuw_ap(bufuw, b, c0)).then_inc(sem_x.h, 16)
                    sync.dma_start(out=vTc[b][:], in_=bufvT[b][:, c0:c0 + SCAN_CH]
                                   ).then_inc(sem_x.h, 16)
                    sync.dma_start(out=bTc[b][:], in_=bufbT[b][:, c0:c0 + SCAN_CH]
                                   ).then_inc(sem_x.h, 16)
                    sem_x.n += 64

            loads_done = sem_x.n
            for eng in ("vector", "scalar", "tensor"):
                def mk(eng=eng):
                    def f(e):
                        e.wait_ge(sem_x.h, loads_done)
                    return f
                getattr(blk, eng)(mk())

            for tl in range(SCAN_CH):
                for b in range(B_LOC):
                    pp = tl % 2
                    ps = ps_sc[b][pp]

                    @blk.tensor
                    def _(t, b=b, tl=tl, ps=ps, guard=sv_done[b]):
                        t.wait_ge(sem_v.h, guard)
                        t.matmul(ps[:, 0:64], ind5b[:],
                                 r55c[b][:, tl * 64:(tl + 1) * 64],
                                 start=True, stop=True)
                        t.matmul(ps[:, 64:66], ind4b[:],
                                 uwc[b][:, tl * 2:(tl + 1) * 2],
                                 start=True, stop=True).then_inc(sem_mm.h, 1)
                        sem_mm.n += 1

                    @blk.vector
                    def _(v, b=b, tl=tl, ps=ps):
                        v.wait_ge(sem_mm.h, sem_mm.n)
                        v.tensor_copy(out=uwsb[b][:], in_=ps[:, 64:66])
                        v.tensor_mul(out=t1s[b][:], in0=S_st[b][:], in1=ps[:, 0:32])
                        v.tensor_reduce(out=retr[b][:], in_=t1s[b][:], axis=AX.X,
                                        op=ALU.add).then_inc(sem_v.h, 1)
                        sem_v.n += 1
                        v.wait_ge(sem_v.h, sem_v.n)
                        v.tensor_sub(out=delta[b][:], in0=vTc[b][:, tl:tl + 1],
                                     in1=retr[b][:]).then_inc(sem_v.h, 1)
                        sem_v.n += 1
                        v.wait_ge(sem_v.h, sem_v.n)
                        v.tensor_scalar(out=u2s[b][:], in0=ps[:, 0:32],
                                        scalar1=delta[b][:], scalar2=None,
                                        op0=ALU.mult).then_inc(sem_v.h, 1)
                        sem_v.n += 1
                        v.wait_ge(sem_v.h, sem_v.n)
                        v.scalar_tensor_tensor(out=spre[b][:], in0=S_st[b][:],
                                               scalar=bTc[b][:, tl:tl + 1],
                                               in1=u2s[b][:], op0=ALU.mult,
                                               op1=ALU.add).then_inc(sem_v.h, 1)
                        sem_v.n += 1

                    @blk.scalar
                    def _(s, b=b):
                        s.wait_ge(sem_v.h, sem_v.n)
                        s.activation(out=s2s[b][:], in_=spre[b][:],
                                     func=AF.Tanh).then_inc(sem_a.h, 1)
                        sem_a.n += 1

                    @blk.vector
                    def _(v, b=b, tl=tl, ps=ps):
                        v.wait_ge(sem_a.h, sem_a.n)
                        v.tensor_sub(out=s2s[b][:], in0=s2s[b][:],
                                     in1=S_st[b][:]).then_inc(sem_v.h, 1)
                        sem_v.n += 1
                        v.wait_ge(sem_v.h, sem_v.n)
                        v.scalar_tensor_tensor(out=S_st[b][:], in0=s2s[b][:],
                                               scalar=uwsb[b][:, 0:1],
                                               in1=S_st[b][:], op0=ALU.mult,
                                               op1=ALU.add).then_inc(sem_v.h, 1)
                        sem_v.n += 1
                        v.wait_ge(sem_v.h, sem_v.n)
                        v.tensor_mul(out=t1s[b][:], in0=S_st[b][:], in1=ps[:, 32:64])
                        v.tensor_reduce(out=sqs[b][:], in_=t1s[b][:], axis=AX.X,
                                        op=ALU.add).then_inc(sem_v.h, 1)
                        sem_v.n += 1
                    sv_done[b] = sem_v.n

                    @blk.scalar
                    def _(s, b=b):
                        s.wait_ge(sem_v.h, sem_v.n)
                        s.activation(out=sgs[b][:], in_=sqs[b][:], func=AF.Sigmoid)
                        s.activation(out=sq2s[b][:], in_=sqs[b][:],
                                     func=AF.Square).then_inc(sem_a.h, 1)
                        sem_a.n += 1

                    @blk.vector
                    def _(v, b=b, tl=tl):
                        v.wait_ge(sem_a.h, sem_a.n)
                        v.tensor_mul(out=sq2s[b][:], in0=sq2s[b][:],
                                     in1=sgs[b][:]).then_inc(sem_v.h, 1)
                        sem_v.n += 1
                        v.wait_ge(sem_v.h, sem_v.n)
                        v.tensor_scalar(out=obuf[b][:, tl:tl + 1], in0=sq2s[b][:],
                                        scalar1=uwsb[b][:, 1:2], scalar2=None,
                                        op0=ALU.mult).then_inc(sem_v.h, 1)
                        sem_v.n += 1

            for b in range(B_LOC):
                @blk.gpsimd
                def _(g, b=b, c0=c0, vv=sem_v.n):
                    g.wait_ge(sem_v.h, vv)
                    g.dma_start(out=bufo[b][:, c0:c0 + SCAN_CH],
                                in_=obuf[b][:]).then_inc(sem_g.h, 16)
                    sem_g.n += 16

            @blk.vector
            def _(v, gg=sem_g.n):
                v.wait_ge(sem_g.h, gg)

        accum_started = False
        for b in range(B_LOC):
            t0 = b * S
            for tc in range(4):
                @blk.gpsimd
                def _(g, b=b, tc=tc, guard=sem_mm.n):
                    g.wait_ge(sem_mm.h, guard)
                    g.dma_start(out=botile[:], in_=bufo[b][:, tc * 512:(tc + 1) * 512]
                                ).then_inc(sem_w.h, 16)
                    sem_w.n += 16

                @blk.tensor
                def _(t, tc=tc, ww=sem_w.n):
                    t.wait_ge(sem_w.h, ww)
                    t.wait_ge(sem_v.h, sem_v.n)
                    t.matmul(ps_aux[0:32, 0:512], p32b[:], botile[:],
                             start=True, stop=True).then_inc(sem_mm.h, 1)
                    sem_mm.n += 1

                @blk.vector
                def _(v):
                    v.wait_ge(sem_mm.h, sem_mm.n)
                    v.tensor_copy(out=celltile[:], in_=ps_aux[0:32, 0:512]
                                  ).then_inc(sem_v.h, 1)
                    sem_v.n += 1

                for och in range(NCH):
                    @blk.tensor
                    def _(t, och=och, tc=tc):
                        t.wait_ge(sem_v.h, sem_v.n)
                        t.wait_ge(sem_a.h, sem_a.n)
                        t.matmul(ps_main[:], wout_sb[:, och * 128:(och + 1) * 128],
                                 celltile[:], start=True, stop=True).then_inc(sem_mm.h, 1)
                        sem_mm.n += 1

                    @blk.scalar
                    def _(s, och=och):
                        s.wait_ge(sem_mm.h, sem_mm.n)
                        s.wait_ge(sem_o.h, sem_o.n)
                        s.activation(out=dtile[:], in_=ps_main[:],
                                     func=AF.Identity, scale=1.0).then_inc(sem_a.h, 1)
                        sem_a.n += 1

                    @blk.gpsimd
                    def _(g, och=och, tc=tc, t0=t0, first=not accum_started):
                        g.wait_ge(sem_a.h, sem_a.n)
                        if first:
                            g.wait_ge(sem_o.h, 16 * NCH)
                        g.dma_start(out=xout[och][:, t0 + tc * 512:t0 + (tc + 1) * 512],
                                    in_=dtile[:], accum_op=ALU.add).then_inc(sem_o.h, 16)
                        sem_o.n += 16
                    accum_started = True

        @blk.sync
        def _(sync):
            sync.wait_ge(sem_o.h, sem_o.n)

    p.es.close()
    return p.nc


def _zchunk_ap(buf55, off):
    from concourse.ap import AP as _AP
    return _AP(tensor=buf55.tensor, offset=off, ap=[[512, 128], [1, 512]])


def _chunk55_ap(buf55, b, c0):
    from concourse.ap import AP as _AP
    return _AP(tensor=buf55.tensor, offset=(b * S + c0) * 320,
               ap=[[64, 5], [320, SCAN_CH], [1, 64]])


def _chunkuw_ap(bufuw, b, c0):
    from concourse.ap import AP as _AP
    return _AP(tensor=bufuw.tensor, offset=(b * S + c0) * 8,
               ap=[[2, 4], [8, SCAN_CH], [1, 2]])


def _buf55_kn_ap(buf55, b, tt):
    from concourse.ap import AP as _AP
    return _AP(tensor=buf55.tensor, offset=(b * S + tt * 128) * 320,
               ap=[[320, 128], [64, 4], [1, 32]])


def _buf55_q_ap(buf55, b, tt):
    from concourse.ap import AP as _AP
    return _AP(tensor=buf55.tensor, offset=(b * S + tt * 128) * 320 + 4 * 64 + 32,
               ap=[[320, 128], [1, 32]])


_NC_CACHE = {}


def _get_nc(name, builder):
    if name not in _NC_CACHE:
        _NC_CACHE[name] = builder()
    return _NC_CACHE[name]


def _pick(inputs, names):
    return {n: np.ascontiguousarray(inputs[n]) for n in names}


def _prep_l1_ffn(inputs):
    m = _pick(inputs, ["L1_ffn_expert_bias"])
    m["rT"] = np.ascontiguousarray(np.transpose(inputs["L1_ffn_router_w"]))
    m["wgT"] = np.ascontiguousarray(np.transpose(inputs["L1_ffn_w_gate"], (0, 2, 1)))
    m["wdT"] = np.ascontiguousarray(np.transpose(inputs["L1_ffn_w_down"], (0, 2, 1)))
    m["wgsT"] = np.ascontiguousarray(np.transpose(inputs["L1_ffn_w_gate_shared"]))
    m["wdsT"] = np.ascontiguousarray(np.transpose(inputs["L1_ffn_w_down_shared"]))
    return m


def _prep_l3_ffn(inputs):
    m = _pick(inputs, ["L3_ffn_biases", "L3_ffn_ln_gamma", "L3_ffn_ln_beta"])
    m["qT"] = np.ascontiguousarray(np.transpose(inputs["L3_ffn_qweights"], (0, 2, 1)))
    m["scTrep"] = np.ascontiguousarray(
        np.repeat(np.transpose(inputs["L3_ffn_scales"], (0, 2, 1)), 16, axis=1))
    m["laT"] = np.ascontiguousarray(np.transpose(inputs["L3_ffn_lora_a"], (0, 2, 1)))
    m["lbT"] = np.ascontiguousarray(np.transpose(inputs["L3_ffn_lora_b"], (0, 2, 1)))
    return m


def _prep_l2_attn(inputs):
    m = _pick(inputs, ["L2_attn_b_beta"])
    m["WinT"] = np.ascontiguousarray(np.transpose(inputs["L2_attn_W_in"]))
    m["projT"] = np.ascontiguousarray(np.concatenate(
        [np.transpose(inputs["L2_attn_W_router"]),
         np.transpose(inputs["L2_attn_W_kv"]),
         np.transpose(inputs["L2_attn_W_beta"]),
         np.transpose(inputs["L2_attn_W_q"])], axis=1))
    m["WoutT"] = np.ascontiguousarray(np.transpose(inputs["L2_attn_W_out"]))
    return m


def host_consts_l2a():
    ind5 = np.zeros((5, 128), np.float32)
    for pp in range(128):
        ind5[pp // 32, pp] = 1.0
    ind5[4, :] = 1.0
    ind4 = np.zeros((4, 128), np.float32)
    for pp in range(128):
        ind4[pp // 32, pp] = 1.0
    p32 = np.zeros((128, 32), np.float32)
    for pp in range(128):
        p32[pp, pp % 32] = 1.0
    return {"C_ind5": ind5, "C_ind4": ind4, "C_p32": p32}


def _prep_l2_ffn(inputs):
    m = _pick(inputs, ["L2_ffn_expert_biases", "L2_ffn_br", "L2_ffn_bs"])
    m["WrT"] = np.ascontiguousarray(np.transpose(inputs["L2_ffn_Wr"], (0, 2, 1)))
    m["WsT"] = np.ascontiguousarray(np.transpose(inputs["L2_ffn_Ws"], (0, 2, 1)))
    m["centT"] = np.ascontiguousarray(np.transpose(inputs["L2_ffn_centroids"]))
    return m


LAUNCHES = [
    ("l0_attn", lambda: build_l0_attn,
     lambda inp: _pick(inp, ["L0_attn_qkv_w", "L0_attn_qkv_b",
                             "L0_attn_proj_w", "L0_attn_proj_b"]),
     lambda: host_consts_l0()),
    ("l0_ffn", lambda: build_l0_ffn,
     lambda inp: _pick(inp, ["L0_ffn_W_r", "L0_ffn_W_k", "L0_ffn_W_v",
                             "L0_ffn_mu_r", "L0_ffn_mu_k"]),
     lambda: {}),
    ("l1_attn", lambda: build_l1_attn,
     lambda inp: _pick(inp, ["L1_attn_wqk", "L1_attn_bqk", "L1_attn_gamma",
                             "L1_attn_beta", "L1_attn_wv", "L1_attn_bv",
                             "L1_attn_rel_bias"]),
     lambda: host_consts_l1()),
    ("l1_ffn", lambda: build_l1_ffn, _prep_l1_ffn, lambda: {}),
    ("l2_attn", lambda: build_l2_attn, _prep_l2_attn, lambda: host_consts_l2a()),
    ("l2_ffn", lambda: build_l2_ffn, _prep_l2_ffn, lambda: {}),
    ("l3_attn", lambda: build_l3_attn,
     lambda inp: _pick(inp, ["L3_attn_Wq", "L3_attn_bq", "L3_attn_Wk", "L3_attn_bk",
                             "L3_attn_Wv", "L3_attn_bv", "L3_attn_Wo", "L3_attn_bo"]),
     lambda: {}),
    ("l3_ffn", lambda: build_l3_ffn, _prep_l3_ffn, lambda: {}),
]

N_STAGES = None


def kernel(**inputs):
    x = np.asarray(inputs["L0_attn_x"], dtype=np.float32)
    shards = []
    for c in range(NCORES):
        xs = x[c * B_LOC:(c + 1) * B_LOC]
        xf = np.transpose(xs, (2, 0, 1)).reshape(D, T_LOC)
        shards.append(np.ascontiguousarray(xf.reshape(NCH, 128, T_LOC)))

    cc = host_consts_common()
    launches = LAUNCHES if N_STAGES is None else LAUNCHES[:N_STAGES]
    for name, get_builder, host_prep, get_consts in launches:
        nc = _get_nc(name, get_builder())
        consts = get_consts()
        wmap = host_prep(inputs)
        in_maps = []
        for c in range(NCORES):
            m = {"x_fm": shards[c]}
            m.update(wmap)
            m.update(cc)
            m.update(consts)
            in_maps.append(m)
        res = run_bass_kernel_spmd(nc, in_maps, core_ids=list(range(NCORES)))
        shards = [np.asarray(r["xout_fm"]) for r in res.results]

    out = np.empty((16, S, D), np.float32)
    for c in range(NCORES):
        xf = shards[c].reshape(D, B_LOC, S)
        out[c * B_LOC:(c + 1) * B_LOC] = np.transpose(xf, (1, 2, 0))
    return out
